# version 21
# baseline (speedup 1.0000x reference)
import numpy as np
from contextlib import ExitStack

import concourse.bass as bass
import concourse.mybir as mybir
from concourse.bass_utils import run_bass_kernel_spmd

F32 = mybir.dt.float32
BF16 = mybir.dt.bfloat16
AF = mybir.ActivationFunctionType
ALU = mybir.AluOpType

NCORES = 8
C = 2048
NTOK = 1024
CT = 16
NDT = 64
EPS_ATTN = 1e-15
EPS_LN = 1e-5


def prep_x(x_full, core):
    xc = np.ascontiguousarray(x_full[0, core * NTOK:(core + 1) * NTOK, :].T)
    return np.ascontiguousarray(
        xc.reshape(CT, 128, NTOK).transpose(1, 0, 2).reshape(128, CT * NTOK))


def prep_wkv(w_qkv):
    wk = w_qkv[2048:4096, :]
    wv = w_qkv[4096:6144, :]
    out = np.empty((8, 128, CT * 512), np.float32)
    for fc in range(8):
        blk = np.concatenate([wk[256 * fc:256 * fc + 256, :],
                              wv[256 * fc:256 * fc + 256, :]], axis=0)
        out[fc] = blk.T.reshape(CT, 128, 512).transpose(1, 0, 2).reshape(128, CT * 512)
    return np.ascontiguousarray(out)


def _lhsT_tiles(w_t):
    M = w_t.shape[1] // 128
    r = w_t.reshape(CT, 128, M, 128)
    return np.ascontiguousarray(r.transpose(2, 1, 0, 3).reshape(M, 128, CT * 128))


def prep_wq(w_qkv):
    return _lhsT_tiles(np.ascontiguousarray(w_qkv[0:2048, :].T))


def prep_wproj(w_proj):
    return _lhsT_tiles(np.ascontiguousarray(w_proj.T))


def prep_wgu(w):
    r = w.reshape(C, CT, 128, 4)
    r = r.transpose(1, 3, 0, 2)
    r = r.reshape(CT, 4, CT, 128, 128)
    r = r.transpose(0, 1, 3, 2, 4)
    return np.ascontiguousarray(r.reshape(NDT, 128, CT * 128))


def prep_wd(w_down):
    r = w_down.reshape(CT, 128, 4, CT, 128)
    r = r.transpose(3, 1, 0, 2, 4)
    return np.ascontiguousarray(r.reshape(CT, 128, NDT * 128))


def prep_bias(b_proj):
    return np.ascontiguousarray(b_proj.reshape(CT, 128).T)


def prep_consts():
    p = np.arange(128)
    mask128 = (p[:, None] // 32 == p[None, :] // 32).astype(np.float32)
    selpack = np.zeros((64, CT * 128), np.float32)
    onesblk = np.zeros((128, CT * 64), np.float32)
    for ct in range(CT):
        for g in range(4):
            h = 4 * ct + g
            selpack[h, 128 * ct + 32 * g:128 * ct + 32 * g + 32] = 1.0
            onesblk[32 * g:32 * g + 32, 64 * ct + h] = 1.0
    return mask128, selpack, onesblk


def unprep_out(o, core):
    return o.reshape(128, CT, NTOK).transpose(1, 0, 2).reshape(C, NTOK).T



class Cnt:
    def __init__(self, handle):
        self.h = handle
        self.v = 0

    def inc(self, instr, n=1):
        instr.then_inc(self.h, n)
        self.v += n
        return self.v


class Waiter:
    def __init__(self):
        self.hw = {}

    def wait(self, eng, key, dep):
        if dep is None:
            return
        if isinstance(dep, list):
            for d in dep:
                self.wait(eng, key, d)
            return
        sem, val = dep
        if val <= 0:
            return
        if self.hw.get((key, id(sem)), -1) >= val:
            return
        eng.wait_ge(sem.h, val)
        self.hw[(key, id(sem))] = val


def build_nc():
    nc = bass.Bass("TRN2", target_bir_lowering=False, debug=False)

    xh = nc.dram_tensor("xh", [128, CT * NTOK], F32, kind="ExternalInput").ap()
    wkv = nc.dram_tensor("wkv", [8, 128, CT * 512], F32, kind="ExternalInput").ap()
    wq = nc.dram_tensor("wq", [CT, 128, CT * 128], F32, kind="ExternalInput").ap()
    wproj = nc.dram_tensor("wproj", [CT, 128, CT * 128], F32, kind="ExternalInput").ap()
    wg = nc.dram_tensor("wg", [NDT, 128, CT * 128], F32, kind="ExternalInput").ap()
    wu = nc.dram_tensor("wu", [NDT, 128, CT * 128], F32, kind="ExternalInput").ap()
    wd = nc.dram_tensor("wd", [CT, 128, NDT * 128], F32, kind="ExternalInput").ap()
    biasin = nc.dram_tensor("biasin", [128, CT], F32, kind="ExternalInput").ap()
    maskin = nc.dram_tensor("maskin", [128, 128], F32, kind="ExternalInput").ap()
    selin = nc.dram_tensor("selin", [64, CT * 128], F32, kind="ExternalInput").ap()
    obin = nc.dram_tensor("obin", [128, CT * 64], F32, kind="ExternalInput").ap()
    outd = nc.dram_tensor("outd", [128, CT * NTOK], F32, kind="ExternalOutput").ap()
    cc_in = nc.dram_tensor("cc_in", [128, 2064], F32)
    cc_t1 = nc.dram_tensor("cc_t1", [128, 2064], F32)
    cc_t2 = nc.dram_tensor("cc_t2", [128, 2064], F32)
    cc_out = nc.dram_tensor("cc_out", [128, 2064], F32)

    es = ExitStack()
    SB = lambda name, shape, dt: es.enter_context(nc.sbuf_tensor(name, shape, dt))
    PS = lambda name: es.enter_context(nc.psum_tensor(name, [128, 512], F32))

    xs = SB("xs", [128, CT * NTOK], BF16)
    arena = SB("arena", [128, NDT * NTOK], BF16)
    wbuf = [SB("wbuf0", [128, 8192], BF16), SB("wbuf1", [128, 8192], BF16)]
    k2 = [SB("k2_0", [128, 256], BF16), SB("k2_1", [128, 256], BF16)]
    lnsq = [SB("lnsq0", [128, 1024], BF16), SB("lnsq1", [128, 1024], BF16)]
    lrow = SB("lrow", [1, 1536], F32)
    lnbf = SB("lnbf", [1, 2048], BF16)
    cpool = SB("cpool", [128, 2], BF16)
    onesrow = SB("onesrow", [1, 128], BF16)
    bias_sb = SB("bias_sb", [128, CT], F32)
    eps1 = SB("eps1", [1, 1], F32)
    eps64 = SB("eps64", [64, 1], F32)
    mask_sb = SB("mask_sb", [128, 128], BF16)

    def atile(t0, ncols=1024):
        return arena[:, t0 * 1024: t0 * 1024 + ncols]

    qF = [atile(t) for t in range(16)]
    outstg = [arena[:, 0:1024].bitcast(F32), arena[:, 1024:2048].bitcast(F32)]
    outstg_g = [arena[:, 2048:3072].bitcast(F32), arena[:, 3072:4096].bitcast(F32)]
    vp2 = [arena[:, 1024:1536], arena[:, 1536:2048]]
    attnT = [atile(16 + t) for t in range(16)]
    selpack = arena[0:64, 48 * 1024:50 * 1024]
    onesblk = atile(50)
    recipb = arena[0:64, 51 * 1024:52 * 1024]
    qks = [atile(52), atile(53), atile(54), atile(55)]
    strips = arena[:, 56 * 1024:58 * 1024]
    stag = arena[:, 58 * 1024: 58 * 1024 + 4128].bitcast(F32)
    hst = [atile(t) for t in range(NDT)]
    ssc = [wbuf[0][:, 4096 + 1024 * j: 4096 + 1024 * (j + 1)] for j in range(4)]
    tsc = [wbuf[1][:, 4096 + 1024 * j: 4096 + 1024 * (j + 1)] for j in range(4)]

    pb = [PS(f"pb{i}") for i in range(8)]

    sem_dx = Cnt(es.enter_context(nc.semaphore("dx")))
    sem_dc = Cnt(es.enter_context(nc.semaphore("dc")))
    sem_dA = Cnt(es.enter_context(nc.semaphore("dA")))
    sem_dB = Cnt(es.enter_context(nc.semaphore("dB")))
    sem_oA = Cnt(es.enter_context(nc.semaphore("oA")))
    sem_oB = Cnt(es.enter_context(nc.semaphore("oB")))
    sem_ds = Cnt(es.enter_context(nc.semaphore("ds")))
    sem_mm = Cnt(es.enter_context(nc.semaphore("mm")))
    sem_ev = Cnt(es.enter_context(nc.semaphore("ev")))
    sem_ac = Cnt(es.enter_context(nc.semaphore("ac")))
    sem_cc = Cnt(es.enter_context(nc.semaphore("cc")))
    sem_gv = Cnt(es.enter_context(nc.semaphore("gv")))
    sem_oG = Cnt(es.enter_context(nc.semaphore("oG")))
    W = Waiter()

    block = es.enter_context(nc.Block())

    def sec(engine_name, fn):
        getattr(block, engine_name)(fn)

    ones_col = cpool[:, 0:1]
    dma_marks = {}
    slot_free = [None, None]

    def s_sync0(e):
        e.dma_start(out=bias_sb[:, :], in_=biasin[:, :]).then_inc(sem_ds.h, 16)
    sem_ds.v += 16
    ds_consts = sem_ds.v
    sec("sync", s_sync0)

    def s_gp0(e):
        e.dma_start(out=xs[:, :], in_=xh[:, :], max_dma_last_dim=2048).then_inc(sem_dx.h, 16)
    sem_dx.v += 16
    dma_marks["x"] = (sem_dx, sem_dx.v)
    sec("gpsimd", s_gp0)

    def s_gp_consts(e):
        e.dma_start(out=selpack, in_=selin[:, :], max_dma_last_dim=2048).then_inc(sem_dc.h, 16)
        e.dma_start(out=onesblk, in_=obin[:, :], max_dma_last_dim=2048).then_inc(sem_dc.h, 16)
        e.dma_start(out=mask_sb[:, :], in_=maskin[:, :]).then_inc(sem_dc.h, 16)
    sem_dc.v += 48
    dma_marks["consts"] = (sem_dc, sem_dc.v)

    def s_vinit(e):
        e.memset(cpool[:, 0:1], 1.0)
        e.memset(onesrow[:, :], 1.0)
        e.memset(eps1[:, :], EPS_LN)
        e.memset(eps64[:, :], EPS_ATTN)
        for b in range(2):
            e.memset(vp2[b][:, 128:256], 1.0)
        e.memset(vp2[0][:, 384:512], 1.0)
        e.memset(vp2[1][:, 384:512], 1.0).then_inc(sem_ev.h, 1)
    sem_ev.v += 1
    ev_init = sem_ev.v
    sec("vector", s_vinit)

    slot_sem = None

    def issue_wdma(key, slot, dst, src, extra=None):
        free = slot_free[slot]
        xtra = extra
        ssem = sem_dA if slot == 0 else sem_dB

        def fn(e):
            W.wait(e, "gp", free)
            W.wait(e, "gp", xtra)
            e.dma_start(out=dst, in_=src, max_dma_last_dim=2048).then_inc(ssem.h, 16)
        ssem.v += 16
        dma_marks[key] = (ssem, ssem.v)
        sec("gpsimd", fn)

    qkb_free = [None, None]
    k2_free = [None, None]
    vp_free = [None, None]
    vkb_free = [None, None]
    vk_ready = [None] * 64
    last_kv_mm = 0

    issue_wdma("wkv0", 0, wbuf[0][:, 0:8192], wkv[0, :, :])
    issue_wdma("wkv1", 1, wbuf[1][:, 0:8192], wkv[1, :, :])
    sec("gpsimd", s_gp_consts)

    def emit_vk(it):
        fc, nt = divmod(it, 8)

        def pe_vk(e, fc=fc, nt=nt, it=it):
            W.wait(e, "pe", vk_ready[it])
            if nt == 0 and fc >= 2:
                W.wait(e, "pe", vkb_free[fc % 2])
            first = (nt == 0)
            last = (nt == 7)
            bp = 2 + 2 * (fc % 2)
            for g in range(2):
                ins = e.matmul(pb[bp + g][:, 0:256],
                               k2[it % 2][:, 128 * g:128 * g + 128],
                               vp2[it % 2][:, 256 * g:256 * g + 256],
                               start=first, stop=last)
            sem_mm.inc(ins)
        sec("tensor", pe_vk)
        k2_free[it % 2] = (sem_mm, sem_mm.v)
        vp_free[it % 2] = (sem_mm, sem_mm.v)
        if nt == 7:
            mmv = sem_mm.v

            def dv_vkevac_fc(e, fc=fc, mmv=mmv):
                W.wait(e, "ev", (sem_mm, mmv))
                bp = 2 + 2 * (fc % 2)
                e.tensor_copy(stag[:, 129 * 2 * fc:129 * 2 * fc + 129],
                              pb[bp][:, 0:129])
                sem_ev.inc(e.tensor_copy(stag[:, 129 * (2 * fc + 1):129 * (2 * fc + 1) + 129],
                                         pb[bp + 1][:, 0:129]))
            sec("vector", dv_vkevac_fc)
            vkb_free[fc % 2] = (sem_ev, sem_ev.v)
        return sem_mm.v

    for fc in range(8):
        for nt in range(8):
            it = fc * 8 + nt
            qb = it % 2

            def pe_kv(e, fc=fc, nt=nt, qb=qb):
                W.wait(e, "pe", dma_marks[f"wkv{fc}"])
                W.wait(e, "pe", dma_marks["x"])
                W.wait(e, "pe", qkb_free[qb])
                for cc in range(CT):
                    ins = e.matmul(pb[qb][:, :],
                                   xs[:, cc * NTOK + nt * 128: cc * NTOK + nt * 128 + 128],
                                   wbuf[fc % 2][:, cc * 512: cc * 512 + 512],
                                   start=(cc == 0), stop=(cc == CT - 1))
                sem_mm.inc(ins)
            sec("tensor", pe_kv)
            mm_chunk = sem_mm.v

            if it > 0:
                last_kv_mm = emit_vk(it - 1)

            def ac_kv(e, qb=qb, it=it, mm_chunk=mm_chunk):
                W.wait(e, "ac", (sem_mm, mm_chunk))
                W.wait(e, "ac", k2_free[it % 2])
                sem_ac.inc(e.activation(k2[it % 2][:, :], pb[qb][:, 0:256], AF.Relu))
            sec("scalar", ac_kv)
            ac_k = sem_ac.v

            def dv_kv(e, qb=qb, it=it, mm_chunk=mm_chunk):
                W.wait(e, "ev", (sem_mm, mm_chunk))
                W.wait(e, "ev", (sem_ev, ev_init))
                W.wait(e, "ev", vp_free[it % 2])
                e.tensor_copy(vp2[it % 2][:, 0:128], pb[qb][:, 256:384])
                sem_ev.inc(e.tensor_copy(vp2[it % 2][:, 256:384], pb[qb][:, 384:512]))
            sec("vector", dv_kv)
            ev_v = sem_ev.v
            qkb_free[qb] = [(sem_ac, ac_k), (sem_ev, ev_v)]
            vk_ready[it] = [(sem_ac, ac_k), (sem_ev, ev_v)]
        slot_free[fc % 2] = (sem_mm, sem_mm.v)
        if fc + 2 < 8:
            issue_wdma(f"wkv{fc + 2}", fc % 2, wbuf[fc % 2][:, 0:8192], wkv[fc + 2, :, :])
    last_kv_mm = emit_vk(63)
    slot_free[7 % 2] = (sem_mm, last_kv_mm)

    issue_wdma("wq0", 0, wbuf[0][:, 0:2048], wq[0, :, :])
    issue_wdma("wq1", 1, wbuf[1][:, 0:2048], wq[1, :, :])

    ev_vkevac = vkb_free[1][1]

    def sy_ccstore(e):
        W.wait(e, "sy", (sem_ev, ev_vkevac))
        e.dma_start(out=cc_in.ap(), in_=stag).then_inc(sem_ds.h, 16)
    sem_ds.v += 16
    ds_ccin = sem_ds.v
    sec("sync", sy_ccstore)

    def gp_cc(e):
        W.wait(e, "gp", (sem_ds, ds_ccin))
        e.collective_compute(
            "AllReduce", ALU.add,
            replica_groups=[list(range(NCORES))],
            ins=[cc_in.ap()],
            outs=[cc_out.ap()],
        ).then_inc(sem_cc.h, 1)
    sem_cc.v = 1
    sec("gpsimd", gp_cc)

    def sy_ccback(e):
        W.wait(e, "sy", (sem_cc, sem_cc.v))
        e.dma_start(out=stag, in_=cc_out.ap()).then_inc(sem_ds.h, 16)
    sem_ds.v += 16
    ds_ccback = sem_ds.v
    sec("sync", sy_ccback)

    qbank_free = [qkb_free[0], qkb_free[1]]
    acq_ft = {}
    for ft in range(CT):
        slot = ft % 2
        for half in range(2):
            idx = ft * 2 + half
            qb = idx % 2

            def pe_q(e, ft=ft, half=half, qb=qb, slot=slot):
                W.wait(e, "pe", dma_marks[f"wq{ft}"])
                W.wait(e, "pe", qbank_free[qb])
                for cc in range(CT):
                    ins = e.matmul(pb[qb][:, :],
                                   wbuf[slot][:, cc * 128: cc * 128 + 128],
                                   xs[:, cc * NTOK + half * 512: cc * NTOK + half * 512 + 512],
                                   start=(cc == 0), stop=(cc == CT - 1))
                sem_mm.inc(ins)
            sec("tensor", pe_q)
            mm_q = sem_mm.v

            def ac_q(e, ft=ft, half=half, qb=qb, mm_q=mm_q):
                W.wait(e, "ac", (sem_mm, mm_q))
                sem_ac.inc(e.activation(qF[ft][:, half * 512: half * 512 + 512],
                                        pb[qb][:, :], AF.Relu))
            sec("scalar", ac_q)
            qbank_free[qb] = (sem_ac, sem_ac.v)
        acq_ft[ft] = (sem_ac, sem_ac.v)
        slot_free[slot] = (sem_mm, sem_mm.v)
        if ft + 2 < CT:
            issue_wdma(f"wq{ft + 2}", slot, wbuf[slot][:, 0:2048], wq[ft + 2, :, :])
    ac_q_done = sem_ac.v
    issue_wdma("wp0", 0, wbuf[0][:, 2048:4096], wproj[0, :, :])
    issue_wdma("wp1", 1, wbuf[1][:, 2048:4096], wproj[1, :, :])

    def gp_wgu01(e):
        e.dma_start(out=wbuf[0][:, 4096:6144], in_=wg[0, :, :],
                    max_dma_last_dim=2048).then_inc(sem_dA.h, 16)
        e.dma_start(out=wbuf[0][:, 6144:8192], in_=wu[0, :, :],
                    max_dma_last_dim=2048).then_inc(sem_dA.h, 16)
        e.dma_start(out=wbuf[1][:, 4096:6144], in_=wg[1, :, :],
                    max_dma_last_dim=2048).then_inc(sem_dA.h, 16)
        e.dma_start(out=wbuf[1][:, 6144:8192], in_=wu[1, :, :],
                    max_dma_last_dim=2048).then_inc(sem_dA.h, 16)
    sem_dA.v += 64
    dma_marks["wgu01"] = (sem_dA, sem_dA.v)
    sec("gpsimd", gp_wgu01)

    def dv_strips(e):
        W.wait(e, "ev", (sem_ds, ds_ccback))
        W.wait(e, "ev", dma_marks["consts"])
        for ct in range(CT):
            ins = e.tensor_tensor(strips[:, 128 * ct:128 * ct + 128],
                                  stag[:, 129 * ct:129 * ct + 128],
                                  mask_sb[:, :], ALU.mult)
        sem_ev.inc(ins)
    sec("vector", dv_strips)
    ev_strips = sem_ev.v

    qks_free = [None, None, None, None]
    ev_qks_ct = {}
    for ct in range(CT):
        def dv_qks(e, ct=ct):
            W.wait(e, "ev", (sem_ds, ds_ccback))
            W.wait(e, "ev", acq_ft[ct])
            W.wait(e, "ev", qks_free[ct % 4])
            sem_ev.inc(e.tensor_scalar(qks[ct % 4][:, :], qF[ct][:, :],
                                       stag[:, 129 * ct + 128:129 * ct + 129], None,
                                       ALU.mult, ALU.bypass))
        sec("vector", dv_qks)
        ev_qks_ct[ct] = (sem_ev, sem_ev.v)

    for ct in range(CT):
        def pe_denom(e, ct=ct):
            W.wait(e, "pe", ev_qks_ct[ct])
            W.wait(e, "pe", dma_marks["consts"])
            if ct == 0:
                W.wait(e, "pe", (sem_ev, ev_vkevac))
            for half in range(2):
                ins = e.matmul(pb[2 + half][0:64, 0:512],
                               onesblk[:, 64 * ct:64 * ct + 64],
                               qks[ct % 4][:, half * 512:half * 512 + 512],
                               start=(ct == 0), stop=(ct == CT - 1))
            sem_mm.inc(ins)
        sec("tensor", pe_denom)
        qks_free[ct % 4] = (sem_mm, sem_mm.v)
    mm_denom = sem_mm.v

    def ac_recip(e):
        W.wait(e, "ac", (sem_mm, mm_denom))
        for half in range(2):
            ins = e.activation(stag[0:64, half * 512:half * 512 + 512],
                               pb[2 + half][0:64, 0:512], AF.Ln, bias=eps64[:, :])
        sem_ac.inc(ins)
        W.wait(e, "ac", (sem_ac, sem_ac.v))
        sem_ac.inc(e.activation(stag[0:64, 0:1024], stag[0:64, 0:1024], AF.Exp, scale=-1.0))
    sem_ac.v += 0
    sec("scalar", ac_recip)
    ac_recip_done = sem_ac.v

    def dv_recipb(e):
        W.wait(e, "ev", (sem_ac, ac_recip_done))
        sem_ev.inc(e.tensor_copy(recipb[:, :], stag[0:64, 0:1024]))
    sec("vector", dv_recipb)
    ev_recipb = sem_ev.v

    rbuf = [lnsq[0][:, 0:512], lnsq[1][:, 0:512]]
    rbuf_free = [None, None]
    bcb_free = [None, None]
    outb_free = [None, None]
    for half in range(2):
        for ct in range(CT):
            i = half * CT + ct

            def pe_bo(e, ct=ct, half=half, i=i):
                W.wait(e, "pe", (sem_ev, ev_recipb))
                W.wait(e, "pe", (sem_ev, ev_strips))
                W.wait(e, "pe", acq_ft[ct])
                W.wait(e, "pe", bcb_free[i % 2])
                W.wait(e, "pe", outb_free[i % 2])
                e.matmul(pb[6 + i % 2][:, 0:512],
                         selpack[:, 128 * ct:128 * ct + 128],
                         recipb[:, half * 512:half * 512 + 512],
                         start=True, stop=True)
                sem_mm.inc(e.matmul(pb[4 + i % 2][:, 0:512],
                                    strips[:, 128 * ct:128 * ct + 128],
                                    qF[ct][:, half * 512:half * 512 + 512],
                                    start=True, stop=True))
            sec("tensor", pe_bo)
            mm_bo = sem_mm.v

            def ac_rb(e, i=i, mm_bo=mm_bo):
                W.wait(e, "ac", (sem_mm, mm_bo))
                W.wait(e, "ac", rbuf_free[i % 2])
                sem_ac.inc(e.activation(rbuf[i % 2], pb[6 + i % 2][:, 0:512],
                                        AF.Relu))
            sec("scalar", ac_rb)
            ac_r = sem_ac.v
            bcb_free[i % 2] = (sem_ac, ac_r)

            def dv_attnT(e, ct=ct, half=half, i=i, mm_bo=mm_bo, ac_r=ac_r):
                W.wait(e, "ev", (sem_mm, mm_bo))
                W.wait(e, "ev", (sem_ac, ac_r))
                sem_ev.inc(e.tensor_tensor(attnT[ct][:, half * 512:half * 512 + 512],
                                           rbuf[i % 2], pb[4 + i % 2][:, 0:512],
                                           ALU.mult))
            sec("vector", dv_attnT)
            rbuf_free[i % 2] = (sem_ev, sem_ev.v)
            outb_free[i % 2] = (sem_ev, sem_ev.v)
    ev_attnT_done = sem_ev.v

    lnsq_free = [None, None]
    lnst = {}
    outstg_free = [None, None]

    def emit_ln_sq(ft, ready):
        def ac_sq(e, ft=ft, ready=ready):
            W.wait(e, "ac", ready)
            W.wait(e, "ac", lnsq_free[ft % 2])
            sem_ac.inc(e.activation(lnsq[ft % 2][:, :],
                                    xs[:, ft * NTOK: ft * NTOK + NTOK], AF.Square))
        sec("scalar", ac_sq)
        lnst[ft] = (sem_ac, sem_ac.v)

    def emit_ln_sums(ft, extra=None):
        def pe_sums(e, ft=ft, extra=extra):
            W.wait(e, "pe", lnst[ft])
            W.wait(e, "pe", (sem_ev, ev_init))
            W.wait(e, "pe", extra)
            for h in range(2):
                e.matmul(pb[4 + 2 * h][0:1, 0:512], ones_col,
                         xs[:, ft * NTOK + h * 512: ft * NTOK + h * 512 + 512],
                         start=(ft == 0), stop=(ft == CT - 1))
                ins = e.matmul(pb[5 + 2 * h][0:1, 0:512], ones_col,
                               lnsq[ft % 2][:, h * 512: h * 512 + 512],
                               start=(ft == 0), stop=(ft == CT - 1))
            sem_mm.inc(ins)
        sec("tensor", pe_sums)
        lnsq_free[ft % 2] = (sem_mm, sem_mm.v)
        return sem_mm.v

    def emit_ln_tail(dest_kind, sums_mark):
        A = lrow[:, 0:512]
        B = lrow[:, 512:1024]
        T = lrow[:, 1024:1536]
        for h in range(2):
            def dv_pre(e, h=h, sums_mark=sums_mark):
                W.wait(e, "ev", (sem_mm, sums_mark))
                e.tensor_scalar(A, pb[4 + 2 * h][0:1, 0:512], 1.0 / C, None,
                                ALU.mult, ALU.bypass)
                sem_ev.inc(e.tensor_scalar(B, pb[5 + 2 * h][0:1, 0:512], 1.0 / C,
                                           None, ALU.mult, ALU.bypass))
                W.wait(e, "ev", (sem_ev, sem_ev.v))
                sem_ev.inc(e.tensor_tensor(T, A, A, ALU.mult))
                W.wait(e, "ev", (sem_ev, sem_ev.v))
                sem_ev.inc(e.tensor_tensor(B, B, T, ALU.subtract))
            sec("vector", dv_pre)
            ev_var = sem_ev.v

            def ac_rstd(e, ev_var=ev_var):
                W.wait(e, "ac", (sem_ev, ev_var))
                sem_ac.inc(e.activation(T, B, AF.Ln, bias=eps1[:, :]))
                W.wait(e, "ac", (sem_ac, sem_ac.v))
                sem_ac.inc(e.activation(B, T, AF.Exp, scale=-0.5))
            sec("scalar", ac_rstd)
            ac_rr = sem_ac.v

            def dv_rows(e, h=h, ac_rr=ac_rr):
                W.wait(e, "ev", (sem_ac, ac_rr))
                sem_ev.inc(e.tensor_tensor(T, A, B, ALU.mult))
                W.wait(e, "ev", (sem_ev, sem_ev.v))
                e.tensor_copy(lnbf[:, 1024 * h:1024 * h + 512], B)
                sem_ev.inc(e.tensor_copy(lnbf[:, 1024 * h + 512:1024 * h + 1024], T))
            sec("vector", dv_rows)
        ev_rows = sem_ev.v

        def pe_bcast_ln(e, ev_rows=ev_rows):
            W.wait(e, "pe", (sem_ev, ev_rows))
            for h in range(2):
                e.matmul(pb[4 + 2 * h][:, 0:512], onesrow[:, :],
                         lnbf[0:1, 1024 * h:1024 * h + 512], start=True, stop=True)
                ins = e.matmul(pb[5 + 2 * h][:, 0:512], onesrow[:, :],
                               lnbf[0:1, 1024 * h + 512:1024 * h + 1024],
                               start=True, stop=True)
            sem_mm.inc(ins)
        sec("tensor", pe_bcast_ln)
        mm_bcast = sem_mm.v

        gstg_free = [None, None]
        half_marks = []
        for h in range(2):
            for ct in range(CT):
                def dv_apply(e, ct=ct, h=h, mm_bcast=mm_bcast,
                             dest_kind=dest_kind):
                    W.wait(e, "ev", (sem_mm, mm_bcast))
                    src = xs[:, ct * NTOK + h * 512: ct * NTOK + h * 512 + 512]
                    scr = lnsq[ct % 2][:, 0:512]
                    e.tensor_tensor(scr, src, pb[4 + 2 * h][:, 0:512], ALU.mult)
                    if dest_kind == "xs":
                        sem_ev.inc(e.tensor_tensor(src, scr,
                                                   pb[5 + 2 * h][:, 0:512],
                                                   ALU.subtract))
                    else:
                        stg = outstg[ct // 2 % 2] if ct % 2 == 0 else outstg_g[ct // 2 % 2]
                        W.wait(e, "ev", outstg_free[ct // 2 % 2] if ct % 2 == 0
                               else gstg_free[ct // 2 % 2])
                        sem_ev.inc(e.tensor_tensor(stg[:, 0:512], scr,
                                                   pb[5 + 2 * h][:, 0:512],
                                                   ALU.subtract))
                sec("vector", dv_apply)
                if dest_kind == "out":
                    ev_ap = sem_ev.v
                    if ct % 2 == 0:
                        osem = sem_oA if ct % 4 == 0 else sem_oB

                        def sy_store(e, ct=ct, h=h, ev_ap=ev_ap, osem=osem):
                            W.wait(e, "sy", (sem_ev, ev_ap))
                            e.dma_start(
                                out=outd[:, ct * NTOK + h * 512: ct * NTOK + h * 512 + 512],
                                in_=outstg[ct // 2 % 2][:, 0:512]).then_inc(osem.h, 16)
                        osem.v += 16
                        sec("sync", sy_store)
                        outstg_free[ct // 2 % 2] = (osem, osem.v)
                    else:
                        def gp_store(e, ct=ct, h=h, ev_ap=ev_ap):
                            W.wait(e, "gp", (sem_ev, ev_ap))
                            e.dma_start(
                                out=outd[:, ct * NTOK + h * 512: ct * NTOK + h * 512 + 512],
                                in_=outstg_g[ct // 2 % 2][:, 0:512]).then_inc(sem_oG.h, 16)
                        sem_oG.v += 16
                        sec("gpsimd", gp_store)
                        gstg_free[ct // 2 % 2] = (sem_oG, sem_oG.v)
            half_marks.append([(sem_ev, sem_ev.v), (sem_gv, sem_gv.v)])
        return half_marks

    projb_free = [None, None]
    for ft in range(CT):
        slot = ft % 2
        for half in range(2):
            i = ft * 2 + half
            qb = i % 2

            def pe_proj(e, ft=ft, half=half, qb=qb, slot=slot):
                W.wait(e, "pe", dma_marks[f"wp{ft}"])
                W.wait(e, "pe", (sem_ev, ev_attnT_done))
                W.wait(e, "pe", projb_free[qb])
                for cc in range(CT):
                    ins = e.matmul(pb[qb][:, :],
                                   wbuf[slot][:, 2048 + cc * 128: 2048 + cc * 128 + 128],
                                   attnT[cc][:, half * 512: half * 512 + 512],
                                   start=(cc == 0), stop=(cc == CT - 1))
                sem_mm.inc(ins)
            sec("tensor", pe_proj)
            mm_p = sem_mm.v

            def dv_proj(e, ft=ft, half=half, qb=qb, mm_p=mm_p):
                W.wait(e, "ev", (sem_mm, mm_p))
                W.wait(e, "ev", (sem_ds, ds_consts))
                dst = xs[:, ft * NTOK + half * 512: ft * NTOK + half * 512 + 512]
                sem_ev.inc(e.scalar_tensor_tensor(dst, pb[qb][:, :], bias_sb[:, ft:ft + 1],
                                                  dst, ALU.add, ALU.add))
            sec("vector", dv_proj)
            projb_free[qb] = (sem_ev, sem_ev.v)
        emit_ln_sq(ft, (sem_ev, sem_ev.v))
        if ft >= 1:
            emit_ln_sums(ft - 1,
                         extra=([bcb_free[0], bcb_free[1]] if ft == 1 else None))
        slot_free[slot] = (sem_mm, sem_mm.v)
        if ft + 2 < CT:
            issue_wdma(f"wp{ft + 2}", slot, wbuf[slot][:, 2048:4096], wproj[ft + 2, :, :])
    mm_sums1 = emit_ln_sums(CT - 1)
    ev_ln1_half = emit_ln_tail("xs", mm_sums1)



    gub_free = [None, None, None, None]
    h_done = {}
    ev_sparsify = {}
    for dt in range(NDT):
        slot = dt % 2
        for half in range(2):
            i = dt * 2 + half
            gi = i % 2

            goff = 4096 if dt < 2 else 0
            uoff = 6144 if dt < 2 else 2048

            def pe_gu(e, dt=dt, half=half, slot=slot, gi=gi, goff=goff, uoff=uoff):
                W.wait(e, "pe", dma_marks["wgu01" if dt < 2 else f"wg{dt}"])
                W.wait(e, "pe", ev_ln1_half[half])
                W.wait(e, "pe", gub_free[gi])
                W.wait(e, "pe", gub_free[2 + gi])
                for cc in range(CT):
                    e.matmul(pb[gi][:, :],
                             wbuf[slot][:, goff + cc * 128: goff + cc * 128 + 128],
                             xs[:, cc * NTOK + half * 512: cc * NTOK + half * 512 + 512],
                             start=(cc == 0), stop=(cc == CT - 1))
                if dt >= 2:
                    W.wait(e, "pe", dma_marks[f"wu{dt}"])
                for cc in range(CT):
                    ins = e.matmul(pb[2 + gi][:, :],
                                   wbuf[slot][:, uoff + cc * 128: uoff + cc * 128 + 128],
                                   xs[:, cc * NTOK + half * 512: cc * NTOK + half * 512 + 512],
                                   start=(cc == 0), stop=(cc == CT - 1))
                sem_mm.inc(ins)
            sec("tensor", pe_gu)
            mm_gu = sem_mm.v

            def ac_silu(e, i=i, gi=gi, mm_gu=mm_gu):
                W.wait(e, "ac", (sem_mm, mm_gu))
                sem_ac.inc(e.activation(lnsq[gi][:, 512:1024], pb[gi][:, :], AF.Silu))
            sec("scalar", ac_silu)
            ac_si = sem_ac.v

            def dv_h(e, dt=dt, half=half, gi=gi, ac_si=ac_si):
                W.wait(e, "ev", (sem_ac, ac_si))
                sem_ev.inc(e.tensor_tensor(hst[dt][:, half * 512: half * 512 + 512],
                                           lnsq[gi][:, 512:1024], pb[2 + gi][:, :], ALU.mult))
            sec("vector", dv_h)
            gub_free[gi] = (sem_ev, sem_ev.v)
            gub_free[2 + gi] = (sem_ev, sem_ev.v)
        h_done[dt] = (sem_ev, sem_ev.v)
        slot_free[slot] = (sem_mm, sem_mm.v)
        if dt + 2 < NDT:
            issue_wdma(f"wg{dt + 2}", slot, wbuf[slot][:, 0:2048], wg[dt + 2, :, :])
            issue_wdma(f"wu{dt + 2}", slot, wbuf[slot][:, 2048:4096], wu[dt + 2, :, :])

        if dt % 4 == 3:
            a = dt // 4

            def dv_sparsify(e, a=a):
                W.wait(e, "ev", h_done[4 * a + 3])
                s, t = ssc, tsc
                hj = [hst[4 * a + j] for j in range(4)]
                for j in range(3):
                    e.tensor_tensor(s[j][:, :], hj[j][:, :], hj[j][:, :], ALU.mult)
                sem_ev.inc(e.tensor_tensor(s[3][:, :], hj[3][:, :], hj[3][:, :], ALU.mult))
                W.wait(e, "ev", (sem_ev, sem_ev.v))
                e.tensor_tensor(t[0][:, :], s[0][:, :], s[1][:, :], ALU.max)
                e.tensor_tensor(t[1][:, :], s[0][:, :], s[1][:, :], ALU.min)
                e.tensor_tensor(t[2][:, :], s[2][:, :], s[3][:, :], ALU.max)
                sem_ev.inc(e.tensor_tensor(t[3][:, :], s[2][:, :], s[3][:, :], ALU.min))
                W.wait(e, "ev", (sem_ev, sem_ev.v))
                e.tensor_tensor(t[1][:, :], t[1][:, :], t[3][:, :], ALU.max)
                sem_ev.inc(e.tensor_tensor(t[0][:, :], t[0][:, :], t[2][:, :], ALU.min))
                W.wait(e, "ev", (sem_ev, sem_ev.v))
                sem_ev.inc(e.tensor_tensor(t[0][:, :], t[0][:, :], t[1][:, :], ALU.max))
                W.wait(e, "ev", (sem_ev, sem_ev.v))
                for j in range(3):
                    e.tensor_tensor(s[j][:, :], s[j][:, :], t[0][:, :], ALU.is_ge)
                sem_ev.inc(e.tensor_tensor(s[3][:, :], s[3][:, :], t[0][:, :], ALU.is_ge))
                W.wait(e, "ev", (sem_ev, sem_ev.v))
                for j in range(3):
                    e.tensor_tensor(hj[j][:, :], hj[j][:, :], s[j][:, :], ALU.mult)
                sem_ev.inc(e.tensor_tensor(hj[3][:, :], hj[3][:, :], s[3][:, :], ALU.mult))
            sec("vector", dv_sparsify)
            ev_sparsify[a] = (sem_ev, sem_ev.v)
    ev_sparsify_done = sem_ev.v

    def gp_wd01(e):
        W.wait(e, "gp", slot_free[0])
        W.wait(e, "gp", slot_free[1])
        e.dma_start(out=wbuf[0][:, 0:4096], in_=wd[0, :, 0:4096],
                    max_dma_last_dim=2048).then_inc(sem_dA.h, 16)
        e.dma_start(out=wbuf[1][:, 0:4096], in_=wd[1, :, 0:4096],
                    max_dma_last_dim=2048).then_inc(sem_dA.h, 16)
    sem_dA.v += 32
    dma_marks["wd01_lo"] = (sem_dA, sem_dA.v)
    sec("gpsimd", gp_wd01)

    def gp_wd01h(e):
        W.wait(e, "gp", (sem_ev, ev_sparsify_done))
        e.dma_start(out=wbuf[0][:, 4096:8192], in_=wd[0, :, 4096:8192],
                    max_dma_last_dim=2048).then_inc(sem_dA.h, 16)
        e.dma_start(out=wbuf[1][:, 4096:8192], in_=wd[1, :, 4096:8192],
                    max_dma_last_dim=2048).then_inc(sem_dA.h, 16)
    sem_dA.v += 32
    dma_marks["wd01_hi"] = (sem_dA, sem_dA.v)
    sec("gpsimd", gp_wd01h)
    dma_marks["wd0"] = dma_marks["wd01_lo"]
    dma_marks["wd1"] = dma_marks["wd01_lo"]

    downb_free = [None, None, None, None]
    lnsq2_free = {}
    ln2_sums_mark = {}
    A2s = lrow[:, 0:512]
    B2s = lrow[:, 512:1024]
    T2s = lrow[:, 1024:1536]
    out_free = {"sA": [None, None], "sG": [None, None]}

    def ln2_sq(ft, h, ready):
        def ac_sq2(e, ft=ft, h=h, ready=ready):
            W.wait(e, "ac", ready)
            W.wait(e, "ac", lnsq2_free.get((ft % 2, h)))
            sem_ac.inc(e.activation(
                lnsq[ft % 2][:, h * 512: h * 512 + 512],
                xs[:, ft * NTOK + h * 512: ft * NTOK + h * 512 + 512],
                AF.Square))
        sec("scalar", ac_sq2)
        lnst[(ft, h)] = (sem_ac, sem_ac.v)

    def ln2_sums(ft, h, extra=None):
        def pe_s2(e, ft=ft, h=h, extra=extra):
            W.wait(e, "pe", lnst[(ft, h)])
            W.wait(e, "pe", extra)
            e.matmul(pb[4 + 2 * h][0:1, 0:512], ones_col,
                     xs[:, ft * NTOK + h * 512: ft * NTOK + h * 512 + 512],
                     start=(ft == 0), stop=(ft == CT - 1))
            ins = e.matmul(pb[5 + 2 * h][0:1, 0:512], ones_col,
                           lnsq[ft % 2][:, h * 512: h * 512 + 512],
                           start=(ft == 0), stop=(ft == CT - 1))
            sem_mm.inc(ins)
        sec("tensor", pe_s2)
        lnsq2_free[(ft % 2, h)] = (sem_mm, sem_mm.v)
        ln2_sums_mark[h] = (sem_mm, sem_mm.v)

    def ln2_stats(h):
        smark = ln2_sums_mark[h]

        def dv_pre2(e, h=h, smark=smark):
            W.wait(e, "ev", smark)
            e.tensor_scalar(A2s, pb[4 + 2 * h][0:1, 0:512], 1.0 / C, None,
                            ALU.mult, ALU.bypass)
            sem_ev.inc(e.tensor_scalar(B2s, pb[5 + 2 * h][0:1, 0:512], 1.0 / C,
                                       None, ALU.mult, ALU.bypass))
            W.wait(e, "ev", (sem_ev, sem_ev.v))
            sem_ev.inc(e.tensor_tensor(T2s, A2s, A2s, ALU.mult))
            W.wait(e, "ev", (sem_ev, sem_ev.v))
            sem_ev.inc(e.tensor_tensor(B2s, B2s, T2s, ALU.subtract))
        sec("vector", dv_pre2)
        ev_var = sem_ev.v

        def ac_rstd2(e, ev_var=ev_var):
            W.wait(e, "ac", (sem_ev, ev_var))
            sem_ac.inc(e.activation(T2s, B2s, AF.Ln, bias=eps1[:, :]))
            W.wait(e, "ac", (sem_ac, sem_ac.v))
            sem_ac.inc(e.activation(B2s, T2s, AF.Exp, scale=-0.5))
        sec("scalar", ac_rstd2)
        ac_rr = sem_ac.v

        def dv_rows2(e, h=h, ac_rr=ac_rr):
            W.wait(e, "ev", (sem_ac, ac_rr))
            sem_ev.inc(e.tensor_tensor(T2s, A2s, B2s, ALU.mult))
            W.wait(e, "ev", (sem_ev, sem_ev.v))
            e.tensor_copy(lnbf[:, 1024 * h:1024 * h + 512], B2s)
            sem_ev.inc(e.tensor_copy(lnbf[:, 1024 * h + 512:1024 * h + 1024], T2s))
        sec("vector", dv_rows2)
        return sem_ev.v

    def ln2_bcast(h, ev_rows):
        def pe_b2(e, h=h, ev_rows=ev_rows):
            W.wait(e, "pe", (sem_ev, ev_rows))
            e.matmul(pb[4 + 2 * h][:, 0:512], onesrow[:, :],
                     lnbf[0:1, 1024 * h:1024 * h + 512], start=True, stop=True)
            ins = e.matmul(pb[5 + 2 * h][:, 0:512], onesrow[:, :],
                           lnbf[0:1, 1024 * h + 512:1024 * h + 1024],
                           start=True, stop=True)
            sem_mm.inc(ins)
        sec("tensor", pe_b2)
        return sem_mm.v

    def ln2_apply(h, mm_b, cts):
        for ct in cts:
            def dv_ap2(e, ct=ct, h=h, mm_b=mm_b):
                W.wait(e, "ev", (sem_mm, mm_b))
                src = xs[:, ct * NTOK + h * 512: ct * NTOK + h * 512 + 512]
                scr = lnsq[ct % 2][:, 0:512]
                e.tensor_tensor(scr, src, pb[4 + 2 * h][:, 0:512], ALU.mult)
                stg = outstg[ct // 2 % 2] if ct % 2 == 0 else outstg_g[ct // 2 % 2]
                W.wait(e, "ev", out_free["sA" if ct % 2 == 0 else "sG"][ct // 2 % 2])
                sem_ev.inc(e.tensor_tensor(stg[:, 0:512], scr,
                                           pb[5 + 2 * h][:, 0:512], ALU.subtract))
            sec("vector", dv_ap2)
            ev_ap = sem_ev.v
            if ct % 2 == 0:
                osem = sem_oA if ct % 4 == 0 else sem_oB

                def sy_st2(e, ct=ct, h=h, ev_ap=ev_ap, osem=osem):
                    W.wait(e, "sy", (sem_ev, ev_ap))
                    e.dma_start(
                        out=outd[:, ct * NTOK + h * 512: ct * NTOK + h * 512 + 512],
                        in_=outstg[ct // 2 % 2][:, 0:512]).then_inc(osem.h, 16)
                osem.v += 16
                sec("sync", sy_st2)
                out_free["sA"][ct // 2 % 2] = (osem, osem.v)
            else:
                def gp_st2(e, ct=ct, h=h, ev_ap=ev_ap):
                    W.wait(e, "gp", (sem_ev, ev_ap))
                    e.dma_start(
                        out=outd[:, ct * NTOK + h * 512: ct * NTOK + h * 512 + 512],
                        in_=outstg_g[ct // 2 % 2][:, 0:512]).then_inc(sem_oG.h, 16)
                sem_oG.v += 16
                sec("gpsimd", gp_st2)
                out_free["sG"][ct // 2 % 2] = (sem_oG, sem_oG.v)

    def pe_down_half(e, ft, slot, b, h, dts):
        if h == 0 and dts[0] == 0:
            W.wait(e, "pe", dma_marks[f"wd{ft}"])
        if dts[0] == 0:
            W.wait(e, "pe", downb_free[b])
        for dt in dts:
            W.wait(e, "pe", ev_sparsify[dt // 4])
            if ft < 2 and dt == 32:
                W.wait(e, "pe", dma_marks["wd01_hi"])
            ins = e.matmul(pb[b][:, :], wbuf[slot][:, dt * 128: dt * 128 + 128],
                           hst[dt][:, h * 512: h * 512 + 512],
                           start=(dt == 0), stop=(dt == NDT - 1))
        sem_mm.inc(ins)

    for ft in range(CT):
        slot = ft % 2
        b0 = 2 * (ft % 2)
        for h in range(2):
            b = b0 + h
            if ft == CT - 1 and h == 1:
                break
            def pe_dh(e, ft=ft, slot=slot, b=b, h=h):
                pe_down_half(e, ft, slot, b, h, list(range(NDT)))
            sec("tensor", pe_dh)
            mm_d = sem_mm.v

            def dv_dh(e, ft=ft, b=b, h=h, mm_d=mm_d):
                W.wait(e, "ev", (sem_mm, mm_d))
                dst = xs[:, ft * NTOK + h * 512: ft * NTOK + h * 512 + 512]
                sem_ev.inc(e.tensor_tensor(dst, pb[b][:, :], dst, ALU.add))
            sec("vector", dv_dh)
            downb_free[b] = (sem_ev, sem_ev.v)
            ln2_sq(ft, h, (sem_ev, sem_ev.v))
            if h == 1:
                ln2_sums(ft, 0, extra=(ev_ln1_half[1] if ft == 0 else None))
            elif ft >= 1:
                ln2_sums(ft - 1, 1)
        if ft == CT - 1:
            break
        slot_free[slot] = (sem_mm, sem_mm.v)
        if ft + 2 < CT:
            issue_wdma(f"wd{ft + 2}", slot, wbuf[slot][:, 0:8192], wd[ft + 2, :, :])

    ln2_sums(CT - 1, 0)
    ev_rows0 = ln2_stats(0)
    b15 = 2 * ((CT - 1) % 2) + 1

    def pe_d15a(e):
        pe_down_half(e, CT - 1, (CT - 1) % 2, b15, 1, list(range(0, 32)))
    sec("tensor", pe_d15a)

    mm_b0 = ln2_bcast(0, ev_rows0)

    def pe_d15b(e):
        pe_down_half(e, CT - 1, (CT - 1) % 2, b15, 1, list(range(32, NDT)))
    sec("tensor", pe_d15b)
    mm_d15 = sem_mm.v

    ln2_apply(0, mm_b0, list(range(6)))

    def dv_d15(e, mm_d15=mm_d15):
        W.wait(e, "ev", (sem_mm, mm_d15))
        dst = xs[:, (CT - 1) * NTOK + 512: (CT - 1) * NTOK + 1024]
        sem_ev.inc(e.tensor_tensor(dst, pb[b15][:, :], dst, ALU.add))
    sec("vector", dv_d15)
    ln2_sq(CT - 1, 1, (sem_ev, sem_ev.v))
    ln2_sums(CT - 1, 1)
    ln2_apply(0, mm_b0, list(range(6, CT)))
    ev_rows1 = ln2_stats(1)
    mm_b1 = ln2_bcast(1, ev_rows1)
    ln2_apply(1, mm_b1, list(range(CT)))

    fA, fB, fG = sem_oA.v, sem_oB.v, sem_oG.v

    def sy_final(e):
        e.wait_ge(sem_oA.h, fA)
        e.wait_ge(sem_oB.h, fB)
        e.wait_ge(sem_oG.h, fG)
    sec("sync", sy_final)

    es.close()
    return nc


_NC_CACHE = None


def _get_nc():
    global _NC_CACHE
    if _NC_CACHE is None:
        _NC_CACHE = build_nc()
    return _NC_CACHE


def kernel(L0_attn_x, L0_attn_w_qkv, L0_attn_w_proj, L0_attn_b_proj,
           L0_ffn_w_gate, L0_ffn_w_up, L0_ffn_w_down):
    x = np.asarray(L0_attn_x, np.float32)
    mask128, selpack, onesblk = prep_consts()
    shared = {
        "wkv": prep_wkv(np.asarray(L0_attn_w_qkv, np.float32)),
        "wq": prep_wq(np.asarray(L0_attn_w_qkv, np.float32)),
        "wproj": prep_wproj(np.asarray(L0_attn_w_proj, np.float32)),
        "wg": prep_wgu(np.asarray(L0_ffn_w_gate, np.float32)),
        "wu": prep_wgu(np.asarray(L0_ffn_w_up, np.float32)),
        "wd": prep_wd(np.asarray(L0_ffn_w_down, np.float32)),
        "biasin": prep_bias(np.asarray(L0_attn_b_proj, np.float32)),
        "maskin": mask128,
        "selin": selpack,
        "obin": onesblk,
    }
    in_maps = []
    for core in range(NCORES):
        m = dict(shared)
        m["xh"] = prep_x(x, core)
        in_maps.append(m)

    nc = _get_nc()
    res = run_bass_kernel_spmd(nc, in_maps, core_ids=list(range(NCORES)))
    rows = [unprep_out(np.asarray(res.results[c]["outd"]), c) for c in range(NCORES)]
    return np.concatenate(rows, axis=0)[None, :, :].astype(np.float32)


# version 22
# speedup vs baseline: 1.0006x; 1.0006x over previous
import numpy as np
from contextlib import ExitStack

import concourse.bass as bass
import concourse.mybir as mybir
from concourse.bass_utils import run_bass_kernel_spmd

F32 = mybir.dt.float32
BF16 = mybir.dt.bfloat16
AF = mybir.ActivationFunctionType
ALU = mybir.AluOpType

NCORES = 8
C = 2048
NTOK = 1024
CT = 16
NDT = 64
EPS_ATTN = 1e-15
EPS_LN = 1e-5


def prep_x(x_full, core):
    xc = np.ascontiguousarray(x_full[0, core * NTOK:(core + 1) * NTOK, :].T)
    return np.ascontiguousarray(
        xc.reshape(CT, 128, NTOK).transpose(1, 0, 2).reshape(128, CT * NTOK))


def prep_wkv(w_qkv):
    wk = w_qkv[2048:4096, :]
    wv = w_qkv[4096:6144, :]
    out = np.empty((8, 128, CT * 512), np.float32)
    for fc in range(8):
        blk = np.concatenate([wk[256 * fc:256 * fc + 256, :],
                              wv[256 * fc:256 * fc + 256, :]], axis=0)
        out[fc] = blk.T.reshape(CT, 128, 512).transpose(1, 0, 2).reshape(128, CT * 512)
    return np.ascontiguousarray(out)


def _lhsT_tiles(w_t):
    M = w_t.shape[1] // 128
    r = w_t.reshape(CT, 128, M, 128)
    return np.ascontiguousarray(r.transpose(2, 1, 0, 3).reshape(M, 128, CT * 128))


def prep_wq(w_qkv):
    return _lhsT_tiles(np.ascontiguousarray(w_qkv[0:2048, :].T))


def prep_wproj(w_proj):
    return _lhsT_tiles(np.ascontiguousarray(w_proj.T))


def prep_wgu(w):
    r = w.reshape(C, CT, 128, 4)
    r = r.transpose(1, 3, 0, 2)
    r = r.reshape(CT, 4, CT, 128, 128)
    r = r.transpose(0, 1, 3, 2, 4)
    return np.ascontiguousarray(r.reshape(NDT, 128, CT * 128))


def prep_wd(w_down):
    r = w_down.reshape(CT, 128, 4, CT, 128)
    r = r.transpose(3, 1, 0, 2, 4)
    return np.ascontiguousarray(r.reshape(CT, 128, NDT * 128))


def prep_bias(b_proj):
    return np.ascontiguousarray(b_proj.reshape(CT, 128).T)


def prep_consts():
    p = np.arange(128)
    mask128 = (p[:, None] // 32 == p[None, :] // 32).astype(np.float32)
    selpack = np.zeros((64, CT * 128), np.float32)
    onesblk = np.zeros((128, CT * 64), np.float32)
    for ct in range(CT):
        for g in range(4):
            h = 4 * ct + g
            selpack[h, 128 * ct + 32 * g:128 * ct + 32 * g + 32] = 1.0
            onesblk[32 * g:32 * g + 32, 64 * ct + h] = 1.0
    return mask128, selpack, onesblk


def unprep_out(o, core):
    return o.reshape(128, CT, NTOK).transpose(1, 0, 2).reshape(C, NTOK).T



class Cnt:
    def __init__(self, handle):
        self.h = handle
        self.v = 0

    def inc(self, instr, n=1):
        instr.then_inc(self.h, n)
        self.v += n
        return self.v


class Waiter:
    def __init__(self):
        self.hw = {}

    def wait(self, eng, key, dep):
        if dep is None:
            return
        if isinstance(dep, list):
            for d in dep:
                self.wait(eng, key, d)
            return
        sem, val = dep
        if val <= 0:
            return
        if self.hw.get((key, id(sem)), -1) >= val:
            return
        eng.wait_ge(sem.h, val)
        self.hw[(key, id(sem))] = val


def build_nc():
    nc = bass.Bass("TRN2", target_bir_lowering=False, debug=False)

    xh = nc.dram_tensor("xh", [128, CT * NTOK], F32, kind="ExternalInput").ap()
    wkv = nc.dram_tensor("wkv", [8, 128, CT * 512], F32, kind="ExternalInput").ap()
    wq = nc.dram_tensor("wq", [CT, 128, CT * 128], F32, kind="ExternalInput").ap()
    wproj = nc.dram_tensor("wproj", [CT, 128, CT * 128], F32, kind="ExternalInput").ap()
    wg = nc.dram_tensor("wg", [NDT, 128, CT * 128], F32, kind="ExternalInput").ap()
    wu = nc.dram_tensor("wu", [NDT, 128, CT * 128], F32, kind="ExternalInput").ap()
    wd = nc.dram_tensor("wd", [CT, 128, NDT * 128], F32, kind="ExternalInput").ap()
    biasin = nc.dram_tensor("biasin", [128, CT], F32, kind="ExternalInput").ap()
    maskin = nc.dram_tensor("maskin", [128, 128], F32, kind="ExternalInput").ap()
    selin = nc.dram_tensor("selin", [64, CT * 128], F32, kind="ExternalInput").ap()
    obin = nc.dram_tensor("obin", [128, CT * 64], F32, kind="ExternalInput").ap()
    outd = nc.dram_tensor("outd", [128, CT * NTOK], F32, kind="ExternalOutput").ap()
    cc_in = nc.dram_tensor("cc_in", [128, 2064], F32)
    cc_t1 = nc.dram_tensor("cc_t1", [128, 2064], F32)
    cc_t2 = nc.dram_tensor("cc_t2", [128, 2064], F32)
    cc_out = nc.dram_tensor("cc_out", [128, 2064], F32)

    es = ExitStack()
    SB = lambda name, shape, dt: es.enter_context(nc.sbuf_tensor(name, shape, dt))
    PS = lambda name: es.enter_context(nc.psum_tensor(name, [128, 512], F32))

    xs = SB("xs", [128, CT * NTOK], BF16)
    arena = SB("arena", [128, NDT * NTOK], BF16)
    wbuf = [SB("wbuf0", [128, 8192], BF16), SB("wbuf1", [128, 8192], BF16)]
    k2 = [SB("k2_0", [128, 256], BF16), SB("k2_1", [128, 256], BF16)]
    lnsq = [SB("lnsq0", [128, 1024], BF16), SB("lnsq1", [128, 1024], BF16)]
    lrow = SB("lrow", [1, 1536], F32)
    lnbf = SB("lnbf", [1, 2048], BF16)
    cpool = SB("cpool", [128, 2], BF16)
    onesrow = SB("onesrow", [1, 128], BF16)
    bias_sb = SB("bias_sb", [128, CT], F32)
    eps1 = SB("eps1", [1, 1], F32)
    eps64 = SB("eps64", [64, 1], F32)
    mask_sb = SB("mask_sb", [128, 128], BF16)

    def atile(t0, ncols=1024):
        return arena[:, t0 * 1024: t0 * 1024 + ncols]

    qF = [atile(t) for t in range(16)]
    outstg = [arena[:, 0:1024].bitcast(F32), arena[:, 1024:2048].bitcast(F32)]
    outstg_g = [arena[:, 2048:3072].bitcast(F32), arena[:, 3072:4096].bitcast(F32)]
    vp2 = [arena[:, 1024:1536], arena[:, 1536:2048]]
    attnT = [atile(16 + t) for t in range(16)]
    selpack = arena[0:64, 48 * 1024:50 * 1024]
    onesblk = atile(50)
    recipb = arena[0:64, 51 * 1024:52 * 1024]
    qks = [atile(52), atile(53), atile(54), atile(55)]
    strips = arena[:, 56 * 1024:58 * 1024]
    stag = arena[:, 58 * 1024: 58 * 1024 + 4128].bitcast(F32)
    hst = [atile(t) for t in range(NDT)]
    ssc = [wbuf[0][:, 4096 + 1024 * j: 4096 + 1024 * (j + 1)] for j in range(4)]
    tsc = [wbuf[1][:, 4096 + 1024 * j: 4096 + 1024 * (j + 1)] for j in range(4)]

    pb = [PS(f"pb{i}") for i in range(8)]

    sem_dx = Cnt(es.enter_context(nc.semaphore("dx")))
    sem_dc = Cnt(es.enter_context(nc.semaphore("dc")))
    sem_dA = Cnt(es.enter_context(nc.semaphore("dA")))
    sem_dB = Cnt(es.enter_context(nc.semaphore("dB")))
    sem_oA = Cnt(es.enter_context(nc.semaphore("oA")))
    sem_oB = Cnt(es.enter_context(nc.semaphore("oB")))
    sem_ds = Cnt(es.enter_context(nc.semaphore("ds")))
    sem_mm = Cnt(es.enter_context(nc.semaphore("mm")))
    sem_ev = Cnt(es.enter_context(nc.semaphore("ev")))
    sem_ac = Cnt(es.enter_context(nc.semaphore("ac")))
    sem_cc = Cnt(es.enter_context(nc.semaphore("cc")))
    sem_gv = Cnt(es.enter_context(nc.semaphore("gv")))
    sem_oG = Cnt(es.enter_context(nc.semaphore("oG")))
    W = Waiter()

    block = es.enter_context(nc.Block())

    def sec(engine_name, fn):
        getattr(block, engine_name)(fn)

    ones_col = cpool[:, 0:1]
    dma_marks = {}
    slot_free = [None, None]

    def s_sync0(e):
        e.dma_start(out=bias_sb[:, :], in_=biasin[:, :]).then_inc(sem_ds.h, 16)
    sem_ds.v += 16
    ds_consts = sem_ds.v
    sec("sync", s_sync0)

    def s_gp0(e):
        e.dma_start(out=xs[:, :], in_=xh[:, :], max_dma_last_dim=2048).then_inc(sem_dx.h, 16)
    sem_dx.v += 16
    dma_marks["x"] = (sem_dx, sem_dx.v)
    sec("gpsimd", s_gp0)

    def s_gp_consts(e):
        e.dma_start(out=selpack, in_=selin[:, :], max_dma_last_dim=2048).then_inc(sem_dc.h, 16)
        e.dma_start(out=onesblk, in_=obin[:, :], max_dma_last_dim=2048).then_inc(sem_dc.h, 16)
        e.dma_start(out=mask_sb[:, :], in_=maskin[:, :]).then_inc(sem_dc.h, 16)
    sem_dc.v += 48
    dma_marks["consts"] = (sem_dc, sem_dc.v)

    def s_vinit(e):
        e.memset(cpool[:, 0:1], 1.0)
        e.memset(onesrow[:, :], 1.0)
        e.memset(eps1[:, :], EPS_LN)
        e.memset(eps64[:, :], EPS_ATTN)
        for b in range(2):
            e.memset(vp2[b][:, 128:256], 1.0)
        e.memset(vp2[0][:, 384:512], 1.0)
        e.memset(vp2[1][:, 384:512], 1.0).then_inc(sem_ev.h, 1)
    sem_ev.v += 1
    ev_init = sem_ev.v
    sec("vector", s_vinit)

    slot_sem = None

    def issue_wdma(key, slot, dst, src, extra=None):
        free = slot_free[slot]
        xtra = extra
        ssem = sem_dA if slot == 0 else sem_dB

        def fn(e):
            W.wait(e, "gp", free)
            W.wait(e, "gp", xtra)
            e.dma_start(out=dst, in_=src, max_dma_last_dim=2048).then_inc(ssem.h, 16)
        ssem.v += 16
        dma_marks[key] = (ssem, ssem.v)
        sec("gpsimd", fn)

    qkb_free = [None, None]
    k2_free = [None, None]
    vp_free = [None, None]
    vkb_free = [None, None]
    vk_ready = [None] * 64
    last_kv_mm = 0

    issue_wdma("wkv0", 0, wbuf[0][:, 0:8192], wkv[0, :, :])
    issue_wdma("wkv1", 1, wbuf[1][:, 0:8192], wkv[1, :, :])
    sec("gpsimd", s_gp_consts)

    def emit_vk(it):
        fc, nt = divmod(it, 8)

        def pe_vk(e, fc=fc, nt=nt, it=it):
            W.wait(e, "pe", vk_ready[it])
            if nt == 0 and fc >= 2:
                W.wait(e, "pe", vkb_free[fc % 2])
            first = (nt == 0)
            last = (nt == 7)
            bp = 2 + 2 * (fc % 2)
            for g in range(2):
                ins = e.matmul(pb[bp + g][:, 0:256],
                               k2[it % 2][:, 128 * g:128 * g + 128],
                               vp2[it % 2][:, 256 * g:256 * g + 256],
                               start=first, stop=last)
            sem_mm.inc(ins)
        sec("tensor", pe_vk)
        k2_free[it % 2] = (sem_mm, sem_mm.v)
        vp_free[it % 2] = (sem_mm, sem_mm.v)
        if nt == 7:
            mmv = sem_mm.v

            def dv_vkevac_fc(e, fc=fc, mmv=mmv):
                W.wait(e, "ev", (sem_mm, mmv))
                bp = 2 + 2 * (fc % 2)
                e.tensor_copy(stag[:, 129 * 2 * fc:129 * 2 * fc + 129],
                              pb[bp][:, 0:129])
                sem_ev.inc(e.tensor_copy(stag[:, 129 * (2 * fc + 1):129 * (2 * fc + 1) + 129],
                                         pb[bp + 1][:, 0:129]))
            sec("vector", dv_vkevac_fc)
            vkb_free[fc % 2] = (sem_ev, sem_ev.v)
        return sem_mm.v

    for fc in range(8):
        for nt in range(8):
            it = fc * 8 + nt
            qb = it % 2

            def pe_kv(e, fc=fc, nt=nt, qb=qb):
                W.wait(e, "pe", dma_marks[f"wkv{fc}"])
                W.wait(e, "pe", dma_marks["x"])
                W.wait(e, "pe", qkb_free[qb])
                for cc in range(CT):
                    ins = e.matmul(pb[qb][:, :],
                                   xs[:, cc * NTOK + nt * 128: cc * NTOK + nt * 128 + 128],
                                   wbuf[fc % 2][:, cc * 512: cc * 512 + 512],
                                   start=(cc == 0), stop=(cc == CT - 1))
                sem_mm.inc(ins)
            sec("tensor", pe_kv)
            mm_chunk = sem_mm.v

            if it > 0:
                last_kv_mm = emit_vk(it - 1)

            def ac_kv(e, qb=qb, it=it, mm_chunk=mm_chunk):
                W.wait(e, "ac", (sem_mm, mm_chunk))
                W.wait(e, "ac", k2_free[it % 2])
                sem_ac.inc(e.activation(k2[it % 2][:, :], pb[qb][:, 0:256], AF.Relu))
            sec("scalar", ac_kv)
            ac_k = sem_ac.v

            def dv_kv(e, qb=qb, it=it, mm_chunk=mm_chunk):
                W.wait(e, "ev", (sem_mm, mm_chunk))
                W.wait(e, "ev", (sem_ev, ev_init))
                W.wait(e, "ev", vp_free[it % 2])
                e.tensor_copy(vp2[it % 2][:, 0:128], pb[qb][:, 256:384])
                sem_ev.inc(e.tensor_copy(vp2[it % 2][:, 256:384], pb[qb][:, 384:512]))
            sec("vector", dv_kv)
            ev_v = sem_ev.v
            qkb_free[qb] = [(sem_ac, ac_k), (sem_ev, ev_v)]
            vk_ready[it] = [(sem_ac, ac_k), (sem_ev, ev_v)]
        slot_free[fc % 2] = (sem_mm, sem_mm.v)
        if fc + 2 < 8:
            issue_wdma(f"wkv{fc + 2}", fc % 2, wbuf[fc % 2][:, 0:8192], wkv[fc + 2, :, :])
    last_kv_mm = emit_vk(63)
    slot_free[7 % 2] = (sem_mm, last_kv_mm)

    issue_wdma("wq0", 0, wbuf[0][:, 0:2048], wq[0, :, :])
    issue_wdma("wq1", 1, wbuf[1][:, 0:2048], wq[1, :, :])

    ev_vkevac = vkb_free[1][1]

    def sy_ccstore(e):
        W.wait(e, "sy", (sem_ev, ev_vkevac))
        e.dma_start(out=cc_in.ap(), in_=stag).then_inc(sem_ds.h, 16)
    sem_ds.v += 16
    ds_ccin = sem_ds.v
    sec("sync", sy_ccstore)

    def gp_cc(e):
        W.wait(e, "gp", (sem_ds, ds_ccin))
        e.collective_compute(
            "AllReduce", ALU.add,
            replica_groups=[list(range(NCORES))],
            ins=[cc_in.ap()],
            outs=[cc_out.ap()],
        ).then_inc(sem_cc.h, 1)
    sem_cc.v = 1
    sec("gpsimd", gp_cc)

    def sy_ccback(e):
        W.wait(e, "sy", (sem_cc, sem_cc.v))
        e.dma_start(out=stag, in_=cc_out.ap()).then_inc(sem_ds.h, 16)
    sem_ds.v += 16
    ds_ccback = sem_ds.v
    sec("sync", sy_ccback)

    qbank_free = [qkb_free[0], qkb_free[1]]
    acq_ft = {}
    for ft in range(CT):
        slot = ft % 2
        for half in range(2):
            idx = ft * 2 + half
            qb = idx % 2

            def pe_q(e, ft=ft, half=half, qb=qb, slot=slot):
                W.wait(e, "pe", dma_marks[f"wq{ft}"])
                W.wait(e, "pe", qbank_free[qb])
                for cc in range(CT):
                    ins = e.matmul(pb[qb][:, :],
                                   wbuf[slot][:, cc * 128: cc * 128 + 128],
                                   xs[:, cc * NTOK + half * 512: cc * NTOK + half * 512 + 512],
                                   start=(cc == 0), stop=(cc == CT - 1))
                sem_mm.inc(ins)
            sec("tensor", pe_q)
            mm_q = sem_mm.v

            def ac_q(e, ft=ft, half=half, qb=qb, mm_q=mm_q):
                W.wait(e, "ac", (sem_mm, mm_q))
                sem_ac.inc(e.activation(qF[ft][:, half * 512: half * 512 + 512],
                                        pb[qb][:, :], AF.Relu))
            sec("scalar", ac_q)
            qbank_free[qb] = (sem_ac, sem_ac.v)
        acq_ft[ft] = (sem_ac, sem_ac.v)
        slot_free[slot] = (sem_mm, sem_mm.v)
        if ft + 2 < CT:
            issue_wdma(f"wq{ft + 2}", slot, wbuf[slot][:, 0:2048], wq[ft + 2, :, :])
    ac_q_done = sem_ac.v
    issue_wdma("wp0", 0, wbuf[0][:, 2048:4096], wproj[0, :, :])
    issue_wdma("wp1", 1, wbuf[1][:, 2048:4096], wproj[1, :, :])

    def gp_wgu01(e):
        e.dma_start(out=wbuf[0][:, 4096:6144], in_=wg[0, :, :],
                    max_dma_last_dim=2048).then_inc(sem_dA.h, 16)
        e.dma_start(out=wbuf[0][:, 6144:8192], in_=wu[0, :, :],
                    max_dma_last_dim=2048).then_inc(sem_dA.h, 16)
        e.dma_start(out=wbuf[1][:, 4096:6144], in_=wg[1, :, :],
                    max_dma_last_dim=2048).then_inc(sem_dA.h, 16)
        e.dma_start(out=wbuf[1][:, 6144:8192], in_=wu[1, :, :],
                    max_dma_last_dim=2048).then_inc(sem_dA.h, 16)
    sem_dA.v += 64
    dma_marks["wgu01"] = (sem_dA, sem_dA.v)
    sec("gpsimd", gp_wgu01)

    def dv_strips(e):
        W.wait(e, "ev", (sem_ds, ds_ccback))
        W.wait(e, "ev", dma_marks["consts"])
        for ct in range(CT):
            ins = e.tensor_tensor(strips[:, 128 * ct:128 * ct + 128],
                                  stag[:, 129 * ct:129 * ct + 128],
                                  mask_sb[:, :], ALU.mult)
        sem_ev.inc(ins)
    sec("vector", dv_strips)
    ev_strips = sem_ev.v

    qks_free = [None, None, None, None]
    ev_qks_ct = {}
    for ct in range(CT):
        def dv_qks(e, ct=ct):
            W.wait(e, "ev", (sem_ds, ds_ccback))
            W.wait(e, "ev", acq_ft[ct])
            W.wait(e, "ev", qks_free[ct % 4])
            sem_ev.inc(e.tensor_scalar(qks[ct % 4][:, :], qF[ct][:, :],
                                       stag[:, 129 * ct + 128:129 * ct + 129], None,
                                       ALU.mult, ALU.bypass))
        sec("vector", dv_qks)
        ev_qks_ct[ct] = (sem_ev, sem_ev.v)

        def pe_denom(e, ct=ct):
            W.wait(e, "pe", ev_qks_ct[ct])
            W.wait(e, "pe", dma_marks["consts"])
            if ct == 0:
                W.wait(e, "pe", (sem_ev, ev_vkevac))
            for half in range(2):
                ins = e.matmul(pb[2 + half][0:64, 0:512],
                               onesblk[:, 64 * ct:64 * ct + 64],
                               qks[ct % 4][:, half * 512:half * 512 + 512],
                               start=(ct == 0), stop=(ct == CT - 1))
            sem_mm.inc(ins)
        sec("tensor", pe_denom)
        qks_free[ct % 4] = (sem_mm, sem_mm.v)
    mm_denom = sem_mm.v

    def ac_recip(e):
        W.wait(e, "ac", (sem_mm, mm_denom))
        for half in range(2):
            ins = e.activation(stag[0:64, half * 512:half * 512 + 512],
                               pb[2 + half][0:64, 0:512], AF.Ln, bias=eps64[:, :])
        sem_ac.inc(ins)
        W.wait(e, "ac", (sem_ac, sem_ac.v))
        sem_ac.inc(e.activation(stag[0:64, 0:1024], stag[0:64, 0:1024], AF.Exp, scale=-1.0))
    sem_ac.v += 0
    sec("scalar", ac_recip)
    ac_recip_done = sem_ac.v

    def dv_recipb(e):
        W.wait(e, "ev", (sem_ac, ac_recip_done))
        sem_ev.inc(e.tensor_copy(recipb[:, :], stag[0:64, 0:1024]))
    sec("vector", dv_recipb)
    ev_recipb = sem_ev.v

    rbuf = [lnsq[0][:, 0:512], lnsq[1][:, 0:512]]
    rbuf_free = [None, None]
    bcb_free = [None, None]
    outb_free = [None, None]
    for half in range(2):
        for ct in range(CT):
            i = half * CT + ct

            def pe_bo(e, ct=ct, half=half, i=i):
                W.wait(e, "pe", (sem_ev, ev_recipb))
                W.wait(e, "pe", (sem_ev, ev_strips))
                W.wait(e, "pe", acq_ft[ct])
                W.wait(e, "pe", bcb_free[i % 2])
                W.wait(e, "pe", outb_free[i % 2])
                e.matmul(pb[6 + i % 2][:, 0:512],
                         selpack[:, 128 * ct:128 * ct + 128],
                         recipb[:, half * 512:half * 512 + 512],
                         start=True, stop=True)
                sem_mm.inc(e.matmul(pb[4 + i % 2][:, 0:512],
                                    strips[:, 128 * ct:128 * ct + 128],
                                    qF[ct][:, half * 512:half * 512 + 512],
                                    start=True, stop=True))
            sec("tensor", pe_bo)
            mm_bo = sem_mm.v

            def ac_rb(e, i=i, mm_bo=mm_bo):
                W.wait(e, "ac", (sem_mm, mm_bo))
                W.wait(e, "ac", rbuf_free[i % 2])
                sem_ac.inc(e.activation(rbuf[i % 2], pb[6 + i % 2][:, 0:512],
                                        AF.Relu))
            sec("scalar", ac_rb)
            ac_r = sem_ac.v
            bcb_free[i % 2] = (sem_ac, ac_r)

            def dv_attnT(e, ct=ct, half=half, i=i, mm_bo=mm_bo, ac_r=ac_r):
                W.wait(e, "ev", (sem_mm, mm_bo))
                W.wait(e, "ev", (sem_ac, ac_r))
                sem_ev.inc(e.tensor_tensor(attnT[ct][:, half * 512:half * 512 + 512],
                                           rbuf[i % 2], pb[4 + i % 2][:, 0:512],
                                           ALU.mult))
            sec("vector", dv_attnT)
            rbuf_free[i % 2] = (sem_ev, sem_ev.v)
            outb_free[i % 2] = (sem_ev, sem_ev.v)
    ev_attnT_done = sem_ev.v

    lnsq_free = [None, None]
    lnst = {}
    outstg_free = [None, None]

    def emit_ln_sq(ft, ready):
        def ac_sq(e, ft=ft, ready=ready):
            W.wait(e, "ac", ready)
            W.wait(e, "ac", lnsq_free[ft % 2])
            sem_ac.inc(e.activation(lnsq[ft % 2][:, :],
                                    xs[:, ft * NTOK: ft * NTOK + NTOK], AF.Square))
        sec("scalar", ac_sq)
        lnst[ft] = (sem_ac, sem_ac.v)

    def emit_ln_sums(ft, extra=None):
        def pe_sums(e, ft=ft, extra=extra):
            W.wait(e, "pe", lnst[ft])
            W.wait(e, "pe", (sem_ev, ev_init))
            W.wait(e, "pe", extra)
            for h in range(2):
                e.matmul(pb[4 + 2 * h][0:1, 0:512], ones_col,
                         xs[:, ft * NTOK + h * 512: ft * NTOK + h * 512 + 512],
                         start=(ft == 0), stop=(ft == CT - 1))
                ins = e.matmul(pb[5 + 2 * h][0:1, 0:512], ones_col,
                               lnsq[ft % 2][:, h * 512: h * 512 + 512],
                               start=(ft == 0), stop=(ft == CT - 1))
            sem_mm.inc(ins)
        sec("tensor", pe_sums)
        lnsq_free[ft % 2] = (sem_mm, sem_mm.v)
        return sem_mm.v

    def emit_ln_tail(dest_kind, sums_mark):
        A = lrow[:, 0:512]
        B = lrow[:, 512:1024]
        T = lrow[:, 1024:1536]
        for h in range(2):
            def dv_pre(e, h=h, sums_mark=sums_mark):
                W.wait(e, "ev", (sem_mm, sums_mark))
                e.tensor_scalar(A, pb[4 + 2 * h][0:1, 0:512], 1.0 / C, None,
                                ALU.mult, ALU.bypass)
                sem_ev.inc(e.tensor_scalar(B, pb[5 + 2 * h][0:1, 0:512], 1.0 / C,
                                           None, ALU.mult, ALU.bypass))
                W.wait(e, "ev", (sem_ev, sem_ev.v))
                sem_ev.inc(e.tensor_tensor(T, A, A, ALU.mult))
                W.wait(e, "ev", (sem_ev, sem_ev.v))
                sem_ev.inc(e.tensor_tensor(B, B, T, ALU.subtract))
            sec("vector", dv_pre)
            ev_var = sem_ev.v

            def ac_rstd(e, ev_var=ev_var):
                W.wait(e, "ac", (sem_ev, ev_var))
                sem_ac.inc(e.activation(T, B, AF.Ln, bias=eps1[:, :]))
                W.wait(e, "ac", (sem_ac, sem_ac.v))
                sem_ac.inc(e.activation(B, T, AF.Exp, scale=-0.5))
            sec("scalar", ac_rstd)
            ac_rr = sem_ac.v

            def dv_rows(e, h=h, ac_rr=ac_rr):
                W.wait(e, "ev", (sem_ac, ac_rr))
                sem_ev.inc(e.tensor_tensor(T, A, B, ALU.mult))
                W.wait(e, "ev", (sem_ev, sem_ev.v))
                e.tensor_copy(lnbf[:, 1024 * h:1024 * h + 512], B)
                sem_ev.inc(e.tensor_copy(lnbf[:, 1024 * h + 512:1024 * h + 1024], T))
            sec("vector", dv_rows)
        ev_rows = sem_ev.v

        def pe_bcast_ln(e, ev_rows=ev_rows):
            W.wait(e, "pe", (sem_ev, ev_rows))
            for h in range(2):
                e.matmul(pb[4 + 2 * h][:, 0:512], onesrow[:, :],
                         lnbf[0:1, 1024 * h:1024 * h + 512], start=True, stop=True)
                ins = e.matmul(pb[5 + 2 * h][:, 0:512], onesrow[:, :],
                               lnbf[0:1, 1024 * h + 512:1024 * h + 1024],
                               start=True, stop=True)
            sem_mm.inc(ins)
        sec("tensor", pe_bcast_ln)
        mm_bcast = sem_mm.v

        gstg_free = [None, None]
        half_marks = []
        for h in range(2):
            for ct in range(CT):
                def dv_apply(e, ct=ct, h=h, mm_bcast=mm_bcast,
                             dest_kind=dest_kind):
                    W.wait(e, "ev", (sem_mm, mm_bcast))
                    src = xs[:, ct * NTOK + h * 512: ct * NTOK + h * 512 + 512]
                    scr = lnsq[ct % 2][:, 0:512]
                    e.tensor_tensor(scr, src, pb[4 + 2 * h][:, 0:512], ALU.mult)
                    if dest_kind == "xs":
                        sem_ev.inc(e.tensor_tensor(src, scr,
                                                   pb[5 + 2 * h][:, 0:512],
                                                   ALU.subtract))
                    else:
                        stg = outstg[ct // 2 % 2] if ct % 2 == 0 else outstg_g[ct // 2 % 2]
                        W.wait(e, "ev", outstg_free[ct // 2 % 2] if ct % 2 == 0
                               else gstg_free[ct // 2 % 2])
                        sem_ev.inc(e.tensor_tensor(stg[:, 0:512], scr,
                                                   pb[5 + 2 * h][:, 0:512],
                                                   ALU.subtract))
                sec("vector", dv_apply)
                if dest_kind == "out":
                    ev_ap = sem_ev.v
                    if ct % 2 == 0:
                        osem = sem_oA if ct % 4 == 0 else sem_oB

                        def sy_store(e, ct=ct, h=h, ev_ap=ev_ap, osem=osem):
                            W.wait(e, "sy", (sem_ev, ev_ap))
                            e.dma_start(
                                out=outd[:, ct * NTOK + h * 512: ct * NTOK + h * 512 + 512],
                                in_=outstg[ct // 2 % 2][:, 0:512]).then_inc(osem.h, 16)
                        osem.v += 16
                        sec("sync", sy_store)
                        outstg_free[ct // 2 % 2] = (osem, osem.v)
                    else:
                        def gp_store(e, ct=ct, h=h, ev_ap=ev_ap):
                            W.wait(e, "gp", (sem_ev, ev_ap))
                            e.dma_start(
                                out=outd[:, ct * NTOK + h * 512: ct * NTOK + h * 512 + 512],
                                in_=outstg_g[ct // 2 % 2][:, 0:512]).then_inc(sem_oG.h, 16)
                        sem_oG.v += 16
                        sec("gpsimd", gp_store)
                        gstg_free[ct // 2 % 2] = (sem_oG, sem_oG.v)
            half_marks.append([(sem_ev, sem_ev.v), (sem_gv, sem_gv.v)])
        return half_marks

    projb_free = [None, None]
    for ft in range(CT):
        slot = ft % 2
        for half in range(2):
            i = ft * 2 + half
            qb = i % 2

            def pe_proj(e, ft=ft, half=half, qb=qb, slot=slot):
                W.wait(e, "pe", dma_marks[f"wp{ft}"])
                W.wait(e, "pe", (sem_ev, ev_attnT_done))
                W.wait(e, "pe", projb_free[qb])
                for cc in range(CT):
                    ins = e.matmul(pb[qb][:, :],
                                   wbuf[slot][:, 2048 + cc * 128: 2048 + cc * 128 + 128],
                                   attnT[cc][:, half * 512: half * 512 + 512],
                                   start=(cc == 0), stop=(cc == CT - 1))
                sem_mm.inc(ins)
            sec("tensor", pe_proj)
            mm_p = sem_mm.v

            def dv_proj(e, ft=ft, half=half, qb=qb, mm_p=mm_p):
                W.wait(e, "ev", (sem_mm, mm_p))
                W.wait(e, "ev", (sem_ds, ds_consts))
                dst = xs[:, ft * NTOK + half * 512: ft * NTOK + half * 512 + 512]
                sem_ev.inc(e.scalar_tensor_tensor(dst, pb[qb][:, :], bias_sb[:, ft:ft + 1],
                                                  dst, ALU.add, ALU.add))
            sec("vector", dv_proj)
            projb_free[qb] = (sem_ev, sem_ev.v)
        emit_ln_sq(ft, (sem_ev, sem_ev.v))
        if ft >= 1:
            emit_ln_sums(ft - 1,
                         extra=([bcb_free[0], bcb_free[1]] if ft == 1 else None))
        slot_free[slot] = (sem_mm, sem_mm.v)
        if ft + 2 < CT:
            issue_wdma(f"wp{ft + 2}", slot, wbuf[slot][:, 2048:4096], wproj[ft + 2, :, :])
    mm_sums1 = emit_ln_sums(CT - 1)
    ev_ln1_half = emit_ln_tail("xs", mm_sums1)



    gub_free = [None, None, None, None]
    h_done = {}
    ev_sparsify = {}
    for dt in range(NDT):
        slot = dt % 2
        for half in range(2):
            i = dt * 2 + half
            gi = i % 2

            goff = 4096 if dt < 2 else 0
            uoff = 6144 if dt < 2 else 2048

            def pe_gu(e, dt=dt, half=half, slot=slot, gi=gi, goff=goff, uoff=uoff):
                W.wait(e, "pe", dma_marks["wgu01" if dt < 2 else f"wg{dt}"])
                W.wait(e, "pe", ev_ln1_half[half])
                W.wait(e, "pe", gub_free[gi])
                W.wait(e, "pe", gub_free[2 + gi])
                for cc in range(CT):
                    e.matmul(pb[gi][:, :],
                             wbuf[slot][:, goff + cc * 128: goff + cc * 128 + 128],
                             xs[:, cc * NTOK + half * 512: cc * NTOK + half * 512 + 512],
                             start=(cc == 0), stop=(cc == CT - 1))
                if dt >= 2:
                    W.wait(e, "pe", dma_marks[f"wu{dt}"])
                for cc in range(CT):
                    ins = e.matmul(pb[2 + gi][:, :],
                                   wbuf[slot][:, uoff + cc * 128: uoff + cc * 128 + 128],
                                   xs[:, cc * NTOK + half * 512: cc * NTOK + half * 512 + 512],
                                   start=(cc == 0), stop=(cc == CT - 1))
                sem_mm.inc(ins)
            sec("tensor", pe_gu)
            mm_gu = sem_mm.v

            def ac_silu(e, i=i, gi=gi, mm_gu=mm_gu):
                W.wait(e, "ac", (sem_mm, mm_gu))
                sem_ac.inc(e.activation(lnsq[gi][:, 512:1024], pb[gi][:, :], AF.Silu))
            sec("scalar", ac_silu)
            ac_si = sem_ac.v

            def dv_h(e, dt=dt, half=half, gi=gi, ac_si=ac_si):
                W.wait(e, "ev", (sem_ac, ac_si))
                sem_ev.inc(e.tensor_tensor(hst[dt][:, half * 512: half * 512 + 512],
                                           lnsq[gi][:, 512:1024], pb[2 + gi][:, :], ALU.mult))
            sec("vector", dv_h)
            gub_free[gi] = (sem_ev, sem_ev.v)
            gub_free[2 + gi] = (sem_ev, sem_ev.v)
        h_done[dt] = (sem_ev, sem_ev.v)
        slot_free[slot] = (sem_mm, sem_mm.v)
        if dt + 2 < NDT:
            issue_wdma(f"wg{dt + 2}", slot, wbuf[slot][:, 0:2048], wg[dt + 2, :, :])
            issue_wdma(f"wu{dt + 2}", slot, wbuf[slot][:, 2048:4096], wu[dt + 2, :, :])

        if dt % 4 == 3:
            a = dt // 4

            def dv_sparsify(e, a=a):
                W.wait(e, "ev", h_done[4 * a + 3])
                s, t = ssc, tsc
                hj = [hst[4 * a + j] for j in range(4)]
                for j in range(3):
                    e.tensor_tensor(s[j][:, :], hj[j][:, :], hj[j][:, :], ALU.mult)
                sem_ev.inc(e.tensor_tensor(s[3][:, :], hj[3][:, :], hj[3][:, :], ALU.mult))
                W.wait(e, "ev", (sem_ev, sem_ev.v))
                e.tensor_tensor(t[0][:, :], s[0][:, :], s[1][:, :], ALU.max)
                e.tensor_tensor(t[1][:, :], s[0][:, :], s[1][:, :], ALU.min)
                e.tensor_tensor(t[2][:, :], s[2][:, :], s[3][:, :], ALU.max)
                sem_ev.inc(e.tensor_tensor(t[3][:, :], s[2][:, :], s[3][:, :], ALU.min))
                W.wait(e, "ev", (sem_ev, sem_ev.v))
                e.tensor_tensor(t[1][:, :], t[1][:, :], t[3][:, :], ALU.max)
                sem_ev.inc(e.tensor_tensor(t[0][:, :], t[0][:, :], t[2][:, :], ALU.min))
                W.wait(e, "ev", (sem_ev, sem_ev.v))
                sem_ev.inc(e.tensor_tensor(t[0][:, :], t[0][:, :], t[1][:, :], ALU.max))
                W.wait(e, "ev", (sem_ev, sem_ev.v))
                for j in range(3):
                    e.tensor_tensor(s[j][:, :], s[j][:, :], t[0][:, :], ALU.is_ge)
                sem_ev.inc(e.tensor_tensor(s[3][:, :], s[3][:, :], t[0][:, :], ALU.is_ge))
                W.wait(e, "ev", (sem_ev, sem_ev.v))
                for j in range(3):
                    e.tensor_tensor(hj[j][:, :], hj[j][:, :], s[j][:, :], ALU.mult)
                sem_ev.inc(e.tensor_tensor(hj[3][:, :], hj[3][:, :], s[3][:, :], ALU.mult))
            sec("vector", dv_sparsify)
            ev_sparsify[a] = (sem_ev, sem_ev.v)
    ev_sparsify_done = sem_ev.v

    def gp_wd01(e):
        W.wait(e, "gp", slot_free[0])
        W.wait(e, "gp", slot_free[1])
        e.dma_start(out=wbuf[0][:, 0:4096], in_=wd[0, :, 0:4096],
                    max_dma_last_dim=2048).then_inc(sem_dA.h, 16)
        e.dma_start(out=wbuf[1][:, 0:4096], in_=wd[1, :, 0:4096],
                    max_dma_last_dim=2048).then_inc(sem_dA.h, 16)
    sem_dA.v += 32
    dma_marks["wd01_lo"] = (sem_dA, sem_dA.v)
    sec("gpsimd", gp_wd01)

    def gp_wd01h(e):
        W.wait(e, "gp", (sem_ev, ev_sparsify_done))
        e.dma_start(out=wbuf[0][:, 4096:8192], in_=wd[0, :, 4096:8192],
                    max_dma_last_dim=2048).then_inc(sem_dA.h, 16)
        e.dma_start(out=wbuf[1][:, 4096:8192], in_=wd[1, :, 4096:8192],
                    max_dma_last_dim=2048).then_inc(sem_dA.h, 16)
    sem_dA.v += 32
    dma_marks["wd01_hi"] = (sem_dA, sem_dA.v)
    sec("gpsimd", gp_wd01h)
    dma_marks["wd0"] = dma_marks["wd01_lo"]
    dma_marks["wd1"] = dma_marks["wd01_lo"]

    downb_free = [None, None, None, None]
    lnsq2_free = {}
    ln2_sums_mark = {}
    A2s = lrow[:, 0:512]
    B2s = lrow[:, 512:1024]
    T2s = lrow[:, 1024:1536]
    out_free = {"sA": [None, None], "sG": [None, None]}

    def ln2_sq(ft, h, ready):
        def ac_sq2(e, ft=ft, h=h, ready=ready):
            W.wait(e, "ac", ready)
            W.wait(e, "ac", lnsq2_free.get((ft % 2, h)))
            sem_ac.inc(e.activation(
                lnsq[ft % 2][:, h * 512: h * 512 + 512],
                xs[:, ft * NTOK + h * 512: ft * NTOK + h * 512 + 512],
                AF.Square))
        sec("scalar", ac_sq2)
        lnst[(ft, h)] = (sem_ac, sem_ac.v)

    def ln2_sums(ft, h, extra=None):
        def pe_s2(e, ft=ft, h=h, extra=extra):
            W.wait(e, "pe", lnst[(ft, h)])
            W.wait(e, "pe", extra)
            e.matmul(pb[4 + 2 * h][0:1, 0:512], ones_col,
                     xs[:, ft * NTOK + h * 512: ft * NTOK + h * 512 + 512],
                     start=(ft == 0), stop=(ft == CT - 1))
            ins = e.matmul(pb[5 + 2 * h][0:1, 0:512], ones_col,
                           lnsq[ft % 2][:, h * 512: h * 512 + 512],
                           start=(ft == 0), stop=(ft == CT - 1))
            sem_mm.inc(ins)
        sec("tensor", pe_s2)
        lnsq2_free[(ft % 2, h)] = (sem_mm, sem_mm.v)
        ln2_sums_mark[h] = (sem_mm, sem_mm.v)

    def ln2_stats(h):
        smark = ln2_sums_mark[h]

        def dv_pre2(e, h=h, smark=smark):
            W.wait(e, "ev", smark)
            e.tensor_scalar(A2s, pb[4 + 2 * h][0:1, 0:512], 1.0 / C, None,
                            ALU.mult, ALU.bypass)
            sem_ev.inc(e.tensor_scalar(B2s, pb[5 + 2 * h][0:1, 0:512], 1.0 / C,
                                       None, ALU.mult, ALU.bypass))
            W.wait(e, "ev", (sem_ev, sem_ev.v))
            sem_ev.inc(e.tensor_tensor(T2s, A2s, A2s, ALU.mult))
            W.wait(e, "ev", (sem_ev, sem_ev.v))
            sem_ev.inc(e.tensor_tensor(B2s, B2s, T2s, ALU.subtract))
        sec("vector", dv_pre2)
        ev_var = sem_ev.v

        def ac_rstd2(e, ev_var=ev_var):
            W.wait(e, "ac", (sem_ev, ev_var))
            sem_ac.inc(e.activation(T2s, B2s, AF.Ln, bias=eps1[:, :]))
            W.wait(e, "ac", (sem_ac, sem_ac.v))
            sem_ac.inc(e.activation(B2s, T2s, AF.Exp, scale=-0.5))
        sec("scalar", ac_rstd2)
        ac_rr = sem_ac.v

        def dv_rows2(e, h=h, ac_rr=ac_rr):
            W.wait(e, "ev", (sem_ac, ac_rr))
            sem_ev.inc(e.tensor_tensor(T2s, A2s, B2s, ALU.mult))
            W.wait(e, "ev", (sem_ev, sem_ev.v))
            e.tensor_copy(lnbf[:, 1024 * h:1024 * h + 512], B2s)
            sem_ev.inc(e.tensor_copy(lnbf[:, 1024 * h + 512:1024 * h + 1024], T2s))
        sec("vector", dv_rows2)
        return sem_ev.v

    def ln2_bcast(h, ev_rows):
        def pe_b2(e, h=h, ev_rows=ev_rows):
            W.wait(e, "pe", (sem_ev, ev_rows))
            e.matmul(pb[4 + 2 * h][:, 0:512], onesrow[:, :],
                     lnbf[0:1, 1024 * h:1024 * h + 512], start=True, stop=True)
            ins = e.matmul(pb[5 + 2 * h][:, 0:512], onesrow[:, :],
                           lnbf[0:1, 1024 * h + 512:1024 * h + 1024],
                           start=True, stop=True)
            sem_mm.inc(ins)
        sec("tensor", pe_b2)
        return sem_mm.v

    def ln2_apply(h, mm_b, cts):
        for ct in cts:
            def dv_ap2(e, ct=ct, h=h, mm_b=mm_b):
                W.wait(e, "ev", (sem_mm, mm_b))
                src = xs[:, ct * NTOK + h * 512: ct * NTOK + h * 512 + 512]
                scr = lnsq[ct % 2][:, 0:512]
                e.tensor_tensor(scr, src, pb[4 + 2 * h][:, 0:512], ALU.mult)
                stg = outstg[ct // 2 % 2] if ct % 2 == 0 else outstg_g[ct // 2 % 2]
                W.wait(e, "ev", out_free["sA" if ct % 2 == 0 else "sG"][ct // 2 % 2])
                sem_ev.inc(e.tensor_tensor(stg[:, 0:512], scr,
                                           pb[5 + 2 * h][:, 0:512], ALU.subtract))
            sec("vector", dv_ap2)
            ev_ap = sem_ev.v
            if ct % 2 == 0:
                osem = sem_oA if ct % 4 == 0 else sem_oB

                def sy_st2(e, ct=ct, h=h, ev_ap=ev_ap, osem=osem):
                    W.wait(e, "sy", (sem_ev, ev_ap))
                    e.dma_start(
                        out=outd[:, ct * NTOK + h * 512: ct * NTOK + h * 512 + 512],
                        in_=outstg[ct // 2 % 2][:, 0:512]).then_inc(osem.h, 16)
                osem.v += 16
                sec("sync", sy_st2)
                out_free["sA"][ct // 2 % 2] = (osem, osem.v)
            else:
                def gp_st2(e, ct=ct, h=h, ev_ap=ev_ap):
                    W.wait(e, "gp", (sem_ev, ev_ap))
                    e.dma_start(
                        out=outd[:, ct * NTOK + h * 512: ct * NTOK + h * 512 + 512],
                        in_=outstg_g[ct // 2 % 2][:, 0:512]).then_inc(sem_oG.h, 16)
                sem_oG.v += 16
                sec("gpsimd", gp_st2)
                out_free["sG"][ct // 2 % 2] = (sem_oG, sem_oG.v)

    def pe_down_half(e, ft, slot, b, h, dts):
        if h == 0 and dts[0] == 0:
            W.wait(e, "pe", dma_marks[f"wd{ft}"])
        if dts[0] == 0:
            W.wait(e, "pe", downb_free[b])
        for dt in dts:
            W.wait(e, "pe", ev_sparsify[dt // 4])
            if ft < 2 and dt == 32:
                W.wait(e, "pe", dma_marks["wd01_hi"])
            ins = e.matmul(pb[b][:, :], wbuf[slot][:, dt * 128: dt * 128 + 128],
                           hst[dt][:, h * 512: h * 512 + 512],
                           start=(dt == 0), stop=(dt == NDT - 1))
        sem_mm.inc(ins)

    for ft in range(CT):
        slot = ft % 2
        b0 = 2 * (ft % 2)
        for h in range(2):
            b = b0 + h
            if ft == CT - 1 and h == 1:
                break
            def pe_dh(e, ft=ft, slot=slot, b=b, h=h):
                pe_down_half(e, ft, slot, b, h, list(range(NDT)))
            sec("tensor", pe_dh)
            mm_d = sem_mm.v

            def dv_dh(e, ft=ft, b=b, h=h, mm_d=mm_d):
                W.wait(e, "ev", (sem_mm, mm_d))
                dst = xs[:, ft * NTOK + h * 512: ft * NTOK + h * 512 + 512]
                sem_ev.inc(e.tensor_tensor(dst, pb[b][:, :], dst, ALU.add))
            sec("vector", dv_dh)
            downb_free[b] = (sem_ev, sem_ev.v)
            ln2_sq(ft, h, (sem_ev, sem_ev.v))
            if h == 1:
                ln2_sums(ft, 0, extra=(ev_ln1_half[1] if ft == 0 else None))
            elif ft >= 1:
                ln2_sums(ft - 1, 1)
        if ft == CT - 1:
            break
        slot_free[slot] = (sem_mm, sem_mm.v)
        if ft + 2 < CT:
            issue_wdma(f"wd{ft + 2}", slot, wbuf[slot][:, 0:8192], wd[ft + 2, :, :])

    ln2_sums(CT - 1, 0)
    ev_rows0 = ln2_stats(0)
    b15 = 2 * ((CT - 1) % 2) + 1

    def pe_d15a(e):
        pe_down_half(e, CT - 1, (CT - 1) % 2, b15, 1, list(range(0, 32)))
    sec("tensor", pe_d15a)

    mm_b0 = ln2_bcast(0, ev_rows0)

    def pe_d15b(e):
        pe_down_half(e, CT - 1, (CT - 1) % 2, b15, 1, list(range(32, NDT)))
    sec("tensor", pe_d15b)
    mm_d15 = sem_mm.v

    ln2_apply(0, mm_b0, list(range(6)))

    def dv_d15(e, mm_d15=mm_d15):
        W.wait(e, "ev", (sem_mm, mm_d15))
        dst = xs[:, (CT - 1) * NTOK + 512: (CT - 1) * NTOK + 1024]
        sem_ev.inc(e.tensor_tensor(dst, pb[b15][:, :], dst, ALU.add))
    sec("vector", dv_d15)
    ln2_sq(CT - 1, 1, (sem_ev, sem_ev.v))
    ln2_sums(CT - 1, 1)
    ln2_apply(0, mm_b0, list(range(6, CT)))
    ev_rows1 = ln2_stats(1)
    mm_b1 = ln2_bcast(1, ev_rows1)
    ln2_apply(1, mm_b1, list(range(CT)))

    fA, fB, fG = sem_oA.v, sem_oB.v, sem_oG.v

    def sy_final(e):
        e.wait_ge(sem_oA.h, fA)
        e.wait_ge(sem_oB.h, fB)
        e.wait_ge(sem_oG.h, fG)
    sec("sync", sy_final)

    es.close()
    return nc


_NC_CACHE = None


def _get_nc():
    global _NC_CACHE
    if _NC_CACHE is None:
        _NC_CACHE = build_nc()
    return _NC_CACHE


def kernel(L0_attn_x, L0_attn_w_qkv, L0_attn_w_proj, L0_attn_b_proj,
           L0_ffn_w_gate, L0_ffn_w_up, L0_ffn_w_down):
    x = np.asarray(L0_attn_x, np.float32)
    mask128, selpack, onesblk = prep_consts()
    shared = {
        "wkv": prep_wkv(np.asarray(L0_attn_w_qkv, np.float32)),
        "wq": prep_wq(np.asarray(L0_attn_w_qkv, np.float32)),
        "wproj": prep_wproj(np.asarray(L0_attn_w_proj, np.float32)),
        "wg": prep_wgu(np.asarray(L0_ffn_w_gate, np.float32)),
        "wu": prep_wgu(np.asarray(L0_ffn_w_up, np.float32)),
        "wd": prep_wd(np.asarray(L0_ffn_w_down, np.float32)),
        "biasin": prep_bias(np.asarray(L0_attn_b_proj, np.float32)),
        "maskin": mask128,
        "selin": selpack,
        "obin": onesblk,
    }
    in_maps = []
    for core in range(NCORES):
        m = dict(shared)
        m["xh"] = prep_x(x, core)
        in_maps.append(m)

    nc = _get_nc()
    res = run_bass_kernel_spmd(nc, in_maps, core_ids=list(range(NCORES)))
    rows = [unprep_out(np.asarray(res.results[c]["outd"]), c) for c in range(NCORES)]
    return np.concatenate(rows, axis=0)[None, :, :].astype(np.float32)


# version 23
# speedup vs baseline: 1.0102x; 1.0096x over previous
import numpy as np
from contextlib import ExitStack

import concourse.bass as bass
import concourse.mybir as mybir
from concourse.bass_utils import run_bass_kernel_spmd

F32 = mybir.dt.float32
BF16 = mybir.dt.bfloat16
AF = mybir.ActivationFunctionType
ALU = mybir.AluOpType

NCORES = 8
C = 2048
NTOK = 1024
CT = 16
NDT = 64
EPS_ATTN = 1e-15
EPS_LN = 1e-5


def prep_x(x_full, core):
    xc = np.ascontiguousarray(x_full[0, core * NTOK:(core + 1) * NTOK, :].T)
    return np.ascontiguousarray(
        xc.reshape(CT, 128, NTOK).transpose(1, 0, 2).reshape(128, CT * NTOK))


def prep_wkv(w_qkv):
    wk = w_qkv[2048:4096, :]
    wv = w_qkv[4096:6144, :]
    out = np.empty((8, 128, CT * 512), np.float32)
    for fc in range(8):
        blk = np.concatenate([wk[256 * fc:256 * fc + 256, :],
                              wv[256 * fc:256 * fc + 256, :]], axis=0)
        out[fc] = blk.T.reshape(CT, 128, 512).transpose(1, 0, 2).reshape(128, CT * 512)
    return np.ascontiguousarray(out)


def _lhsT_tiles(w_t):
    M = w_t.shape[1] // 128
    r = w_t.reshape(CT, 128, M, 128)
    return np.ascontiguousarray(r.transpose(2, 1, 0, 3).reshape(M, 128, CT * 128))


def prep_wq(w_qkv):
    return _lhsT_tiles(np.ascontiguousarray(w_qkv[0:2048, :].T))


def prep_wproj(w_proj):
    return _lhsT_tiles(np.ascontiguousarray(w_proj.T))


def prep_wgu(w):
    r = w.reshape(C, CT, 128, 4)
    r = r.transpose(1, 3, 0, 2)
    r = r.reshape(CT, 4, CT, 128, 128)
    r = r.transpose(0, 1, 3, 2, 4)
    return np.ascontiguousarray(r.reshape(NDT, 128, CT * 128))


def prep_wd(w_down):
    r = w_down.reshape(CT, 128, 4, CT, 128)
    r = r.transpose(3, 1, 0, 2, 4)
    return np.ascontiguousarray(r.reshape(CT, 128, NDT * 128))


def prep_bias(b_proj):
    return np.ascontiguousarray(b_proj.reshape(CT, 128).T)


def prep_consts():
    p = np.arange(128)
    mask128 = (p[:, None] // 32 == p[None, :] // 32).astype(np.float32)
    selpack = np.zeros((64, CT * 128), np.float32)
    onesblk = np.zeros((128, CT * 64), np.float32)
    for ct in range(CT):
        for g in range(4):
            h = 4 * ct + g
            selpack[h, 128 * ct + 32 * g:128 * ct + 32 * g + 32] = 1.0
            onesblk[32 * g:32 * g + 32, 64 * ct + h] = 1.0
    return mask128, selpack, onesblk


def unprep_out(o, core):
    return o.reshape(128, CT, NTOK).transpose(1, 0, 2).reshape(C, NTOK).T



class Cnt:
    def __init__(self, handle):
        self.h = handle
        self.v = 0

    def inc(self, instr, n=1):
        instr.then_inc(self.h, n)
        self.v += n
        return self.v


class Waiter:
    def __init__(self):
        self.hw = {}

    def wait(self, eng, key, dep):
        if dep is None:
            return
        if isinstance(dep, list):
            for d in dep:
                self.wait(eng, key, d)
            return
        sem, val = dep
        if val <= 0:
            return
        if self.hw.get((key, id(sem)), -1) >= val:
            return
        eng.wait_ge(sem.h, val)
        self.hw[(key, id(sem))] = val


def build_nc():
    nc = bass.Bass("TRN2", target_bir_lowering=False, debug=False)

    xh = nc.dram_tensor("xh", [128, CT * NTOK], F32, kind="ExternalInput").ap()
    wkv = nc.dram_tensor("wkv", [8, 128, CT * 512], F32, kind="ExternalInput").ap()
    wq = nc.dram_tensor("wq", [CT, 128, CT * 128], F32, kind="ExternalInput").ap()
    wproj = nc.dram_tensor("wproj", [CT, 128, CT * 128], F32, kind="ExternalInput").ap()
    wg = nc.dram_tensor("wg", [NDT, 128, CT * 128], F32, kind="ExternalInput").ap()
    wu = nc.dram_tensor("wu", [NDT, 128, CT * 128], F32, kind="ExternalInput").ap()
    wd = nc.dram_tensor("wd", [CT, 128, NDT * 128], F32, kind="ExternalInput").ap()
    biasin = nc.dram_tensor("biasin", [128, CT], F32, kind="ExternalInput").ap()
    maskin = nc.dram_tensor("maskin", [128, 128], F32, kind="ExternalInput").ap()
    selin = nc.dram_tensor("selin", [64, CT * 128], F32, kind="ExternalInput").ap()
    obin = nc.dram_tensor("obin", [128, CT * 64], F32, kind="ExternalInput").ap()
    outd = nc.dram_tensor("outd", [128, CT * NTOK], F32, kind="ExternalOutput").ap()
    cc_in = nc.dram_tensor("cc_in", [128, 2064], F32)
    cc_t1 = nc.dram_tensor("cc_t1", [128, 2064], F32)
    cc_t2 = nc.dram_tensor("cc_t2", [128, 2064], F32)
    cc_out = nc.dram_tensor("cc_out", [128, 2064], F32)

    es = ExitStack()
    SB = lambda name, shape, dt: es.enter_context(nc.sbuf_tensor(name, shape, dt))
    PS = lambda name: es.enter_context(nc.psum_tensor(name, [128, 512], F32))

    xs = SB("xs", [128, CT * NTOK], BF16)
    arena = SB("arena", [128, NDT * NTOK], BF16)
    wbuf = [SB("wbuf0", [128, 8192], BF16), SB("wbuf1", [128, 8192], BF16)]
    k2 = [SB("k2_0", [128, 256], BF16), SB("k2_1", [128, 256], BF16)]
    lnsq = [SB("lnsq0", [128, 1024], BF16), SB("lnsq1", [128, 1024], BF16)]
    lrow = SB("lrow", [1, 1536], F32)
    lnbf = SB("lnbf", [1, 2048], BF16)
    cpool = SB("cpool", [128, 2], BF16)
    onesrow = SB("onesrow", [1, 128], BF16)
    bias_sb = SB("bias_sb", [128, CT], F32)
    eps1 = SB("eps1", [1, 1], F32)
    eps64 = SB("eps64", [64, 1], F32)
    mask_sb = SB("mask_sb", [128, 128], BF16)

    def atile(t0, ncols=1024):
        return arena[:, t0 * 1024: t0 * 1024 + ncols]

    qF = [atile(t) for t in range(16)]
    outstg = [arena[:, 0:1024].bitcast(F32), arena[:, 1024:2048].bitcast(F32)]
    outstg_g = [arena[:, 2048:3072].bitcast(F32), arena[:, 3072:4096].bitcast(F32)]
    vp2 = [arena[:, 1024:1536], arena[:, 1536:2048]]
    attnT = [atile(16 + t) for t in range(16)]
    selpack = arena[0:64, 48 * 1024:50 * 1024]
    onesblk = atile(50)
    recipb = arena[0:64, 51 * 1024:52 * 1024]
    qks = [atile(52), atile(53), atile(54), atile(55)]
    strips = arena[:, 56 * 1024:58 * 1024]
    stag = arena[:, 58 * 1024: 58 * 1024 + 4128].bitcast(F32)
    hst = [atile(t) for t in range(NDT)]
    ssc = [wbuf[0][:, 4096 + 1024 * j: 4096 + 1024 * (j + 1)] for j in range(4)]
    tsc = [wbuf[1][:, 4096 + 1024 * j: 4096 + 1024 * (j + 1)] for j in range(4)]

    pb = [PS(f"pb{i}") for i in range(8)]

    sem_dx = Cnt(es.enter_context(nc.semaphore("dx")))
    sem_dc = Cnt(es.enter_context(nc.semaphore("dc")))
    sem_dA = Cnt(es.enter_context(nc.semaphore("dA")))
    sem_dB = Cnt(es.enter_context(nc.semaphore("dB")))
    sem_oA = Cnt(es.enter_context(nc.semaphore("oA")))
    sem_oB = Cnt(es.enter_context(nc.semaphore("oB")))
    sem_ds = Cnt(es.enter_context(nc.semaphore("ds")))
    sem_mm = Cnt(es.enter_context(nc.semaphore("mm")))
    sem_ev = Cnt(es.enter_context(nc.semaphore("ev")))
    sem_ac = Cnt(es.enter_context(nc.semaphore("ac")))
    sem_cc = Cnt(es.enter_context(nc.semaphore("cc")))
    sem_gv = Cnt(es.enter_context(nc.semaphore("gv")))
    sem_oG = Cnt(es.enter_context(nc.semaphore("oG")))
    W = Waiter()

    block = es.enter_context(nc.Block())

    def sec(engine_name, fn):
        getattr(block, engine_name)(fn)

    ones_col = cpool[:, 0:1]
    dma_marks = {}
    slot_free = [None, None]

    def s_sync0(e):
        e.dma_start(out=bias_sb[:, :], in_=biasin[:, :]).then_inc(sem_ds.h, 16)
    sem_ds.v += 16
    ds_consts = sem_ds.v
    sec("sync", s_sync0)

    def s_gp0(e):
        e.dma_start(out=xs[:, :], in_=xh[:, :], max_dma_last_dim=2048).then_inc(sem_dx.h, 16)
    sem_dx.v += 16
    dma_marks["x"] = (sem_dx, sem_dx.v)
    sec("gpsimd", s_gp0)

    def s_gp_consts(e):
        e.dma_start(out=selpack, in_=selin[:, :], max_dma_last_dim=2048).then_inc(sem_dc.h, 16)
        e.dma_start(out=onesblk, in_=obin[:, :], max_dma_last_dim=2048).then_inc(sem_dc.h, 16)
        e.dma_start(out=mask_sb[:, :], in_=maskin[:, :]).then_inc(sem_dc.h, 16)
    sem_dc.v += 48
    dma_marks["consts"] = (sem_dc, sem_dc.v)

    def s_vinit(e):
        e.memset(cpool[:, 0:1], 1.0)
        e.memset(onesrow[:, :], 1.0)
        e.memset(eps1[:, :], EPS_LN)
        e.memset(eps64[:, :], EPS_ATTN)
        for b in range(2):
            e.memset(vp2[b][:, 128:256], 1.0)
        e.memset(vp2[0][:, 384:512], 1.0)
        e.memset(vp2[1][:, 384:512], 1.0).then_inc(sem_ev.h, 1)
    sem_ev.v += 1
    ev_init = sem_ev.v
    sec("vector", s_vinit)

    slot_sem = None

    def issue_wdma(key, slot, dst, src, extra=None):
        free = slot_free[slot]
        xtra = extra
        ssem = sem_dA if slot == 0 else sem_dB

        def fn(e):
            W.wait(e, "gp", free)
            W.wait(e, "gp", xtra)
            e.dma_start(out=dst, in_=src, max_dma_last_dim=2048).then_inc(ssem.h, 16)
        ssem.v += 16
        dma_marks[key] = (ssem, ssem.v)
        sec("gpsimd", fn)

    qkb_free = [None, None]
    k2_free = [None, None]
    vp_free = [None, None]
    vkb_free = [None, None]
    vk_ready = [None] * 64
    last_kv_mm = 0

    issue_wdma("wkv0", 0, wbuf[0][:, 0:8192], wkv[0, :, :])
    issue_wdma("wkv1", 1, wbuf[1][:, 0:8192], wkv[1, :, :])
    sec("gpsimd", s_gp_consts)

    def emit_vk(it):
        fc, nt = divmod(it, 8)

        def pe_vk(e, fc=fc, nt=nt, it=it):
            W.wait(e, "pe", vk_ready[it])
            if nt == 0 and fc >= 2:
                W.wait(e, "pe", vkb_free[fc % 2])
            first = (nt == 0)
            last = (nt == 7)
            bp = 2 + 2 * (fc % 2)
            for g in range(2):
                ins = e.matmul(pb[bp + g][:, 0:256],
                               k2[it % 2][:, 128 * g:128 * g + 128],
                               vp2[it % 2][:, 256 * g:256 * g + 256],
                               start=first, stop=last)
            sem_mm.inc(ins)
        sec("tensor", pe_vk)
        k2_free[it % 2] = (sem_mm, sem_mm.v)
        vp_free[it % 2] = (sem_mm, sem_mm.v)
        if nt == 7:
            mmv = sem_mm.v

            def dv_vkevac_fc(e, fc=fc, mmv=mmv):
                W.wait(e, "ev", (sem_mm, mmv))
                bp = 2 + 2 * (fc % 2)
                e.tensor_copy(stag[:, 129 * 2 * fc:129 * 2 * fc + 129],
                              pb[bp][:, 0:129])
                sem_ev.inc(e.tensor_copy(stag[:, 129 * (2 * fc + 1):129 * (2 * fc + 1) + 129],
                                         pb[bp + 1][:, 0:129]))
            sec("vector", dv_vkevac_fc)
            vkb_free[fc % 2] = (sem_ev, sem_ev.v)
        return sem_mm.v

    for fc in range(8):
        for nt in range(8):
            it = fc * 8 + nt
            qb = it % 2

            def pe_kv(e, fc=fc, nt=nt, qb=qb):
                W.wait(e, "pe", dma_marks[f"wkv{fc}"])
                W.wait(e, "pe", dma_marks["x"])
                W.wait(e, "pe", qkb_free[qb])
                for cc in range(CT):
                    ins = e.matmul(pb[qb][:, :],
                                   xs[:, cc * NTOK + nt * 128: cc * NTOK + nt * 128 + 128],
                                   wbuf[fc % 2][:, cc * 512: cc * 512 + 512],
                                   start=(cc == 0), stop=(cc == CT - 1))
                sem_mm.inc(ins)
            sec("tensor", pe_kv)
            mm_chunk = sem_mm.v

            if it > 0:
                last_kv_mm = emit_vk(it - 1)

            def ac_kv(e, qb=qb, it=it, mm_chunk=mm_chunk):
                W.wait(e, "ac", (sem_mm, mm_chunk))
                W.wait(e, "ac", k2_free[it % 2])
                sem_ac.inc(e.activation(k2[it % 2][:, :], pb[qb][:, 0:256], AF.Relu))
            sec("scalar", ac_kv)
            ac_k = sem_ac.v

            def dv_kv(e, qb=qb, it=it, mm_chunk=mm_chunk):
                W.wait(e, "ev", (sem_mm, mm_chunk))
                W.wait(e, "ev", (sem_ev, ev_init))
                W.wait(e, "ev", vp_free[it % 2])
                e.tensor_copy(vp2[it % 2][:, 0:128], pb[qb][:, 256:384])
                sem_ev.inc(e.tensor_copy(vp2[it % 2][:, 256:384], pb[qb][:, 384:512]))
            sec("vector", dv_kv)
            ev_v = sem_ev.v
            qkb_free[qb] = [(sem_ac, ac_k), (sem_ev, ev_v)]
            vk_ready[it] = [(sem_ac, ac_k), (sem_ev, ev_v)]
        slot_free[fc % 2] = (sem_mm, sem_mm.v)
        if fc + 2 < 8:
            issue_wdma(f"wkv{fc + 2}", fc % 2, wbuf[fc % 2][:, 0:8192], wkv[fc + 2, :, :])
    last_kv_mm = emit_vk(63)
    slot_free[7 % 2] = (sem_mm, last_kv_mm)

    issue_wdma("wq0", 0, wbuf[0][:, 0:2048], wq[0, :, :])
    issue_wdma("wq1", 1, wbuf[1][:, 0:2048], wq[1, :, :])

    ev_vkevac = vkb_free[1][1]

    def sy_ccstore(e):
        W.wait(e, "sy", (sem_ev, ev_vkevac))
        e.dma_start(out=cc_in.ap(), in_=stag).then_inc(sem_ds.h, 16)
    sem_ds.v += 16
    ds_ccin = sem_ds.v
    sec("sync", sy_ccstore)

    def gp_cc(e):
        W.wait(e, "gp", (sem_ds, ds_ccin))
        e.collective_compute(
            "AllReduce", ALU.add,
            replica_groups=[list(range(NCORES))],
            ins=[cc_in.ap()],
            outs=[cc_out.ap()],
        ).then_inc(sem_cc.h, 1)
    sem_cc.v = 1
    sec("gpsimd", gp_cc)

    def sy_ccback(e):
        W.wait(e, "sy", (sem_cc, sem_cc.v))
        e.dma_start(out=stag, in_=cc_out.ap()).then_inc(sem_ds.h, 16)
    sem_ds.v += 16
    ds_ccback = sem_ds.v
    sec("sync", sy_ccback)

    qbank_free = [qkb_free[0], qkb_free[1]]
    acq_ft = {}
    for ft in range(CT):
        slot = ft % 2
        for half in range(2):
            idx = ft * 2 + half
            qb = idx % 2

            def pe_q(e, ft=ft, half=half, qb=qb, slot=slot):
                W.wait(e, "pe", dma_marks[f"wq{ft}"])
                W.wait(e, "pe", qbank_free[qb])
                for cc in range(CT):
                    ins = e.matmul(pb[qb][:, :],
                                   wbuf[slot][:, cc * 128: cc * 128 + 128],
                                   xs[:, cc * NTOK + half * 512: cc * NTOK + half * 512 + 512],
                                   start=(cc == 0), stop=(cc == CT - 1))
                sem_mm.inc(ins)
            sec("tensor", pe_q)
            mm_q = sem_mm.v

            def ac_q(e, ft=ft, half=half, qb=qb, mm_q=mm_q):
                W.wait(e, "ac", (sem_mm, mm_q))
                sem_ac.inc(e.activation(qF[ft][:, half * 512: half * 512 + 512],
                                        pb[qb][:, :], AF.Relu))
            sec("scalar", ac_q)
            qbank_free[qb] = (sem_ac, sem_ac.v)
        acq_ft[ft] = (sem_ac, sem_ac.v)
        slot_free[slot] = (sem_mm, sem_mm.v)
        if ft + 2 < CT:
            issue_wdma(f"wq{ft + 2}", slot, wbuf[slot][:, 0:2048], wq[ft + 2, :, :])
    ac_q_done = sem_ac.v
    issue_wdma("wp0", 0, wbuf[0][:, 2048:4096], wproj[0, :, :])
    issue_wdma("wp1", 1, wbuf[1][:, 2048:4096], wproj[1, :, :])

    def gp_wgu01(e):
        e.dma_start(out=wbuf[0][:, 4096:6144], in_=wg[0, :, :],
                    max_dma_last_dim=2048).then_inc(sem_dc.h, 16)
        e.dma_start(out=wbuf[0][:, 6144:8192], in_=wu[0, :, :],
                    max_dma_last_dim=2048).then_inc(sem_dc.h, 16)
        e.dma_start(out=wbuf[1][:, 4096:6144], in_=wg[1, :, :],
                    max_dma_last_dim=2048).then_inc(sem_dc.h, 16)
        e.dma_start(out=wbuf[1][:, 6144:8192], in_=wu[1, :, :],
                    max_dma_last_dim=2048).then_inc(sem_dc.h, 16)
    sem_dc.v += 64
    dma_marks["wgu01"] = (sem_dc, sem_dc.v)
    sec("gpsimd", gp_wgu01)

    def dv_strips(e):
        W.wait(e, "ev", (sem_ds, ds_ccback))
        W.wait(e, "ev", dma_marks["consts"])
        for ct in range(CT):
            ins = e.tensor_tensor(strips[:, 128 * ct:128 * ct + 128],
                                  stag[:, 129 * ct:129 * ct + 128],
                                  mask_sb[:, :], ALU.mult)
        sem_ev.inc(ins)
    sec("vector", dv_strips)
    ev_strips = sem_ev.v

    qks_free = [None, None, None, None]
    ev_qks_ct = {}
    for ct in range(CT):
        def dv_qks(e, ct=ct):
            W.wait(e, "ev", (sem_ds, ds_ccback))
            W.wait(e, "ev", acq_ft[ct])
            W.wait(e, "ev", qks_free[ct % 4])
            sem_ev.inc(e.tensor_scalar(qks[ct % 4][:, :], qF[ct][:, :],
                                       stag[:, 129 * ct + 128:129 * ct + 129], None,
                                       ALU.mult, ALU.bypass))
        sec("vector", dv_qks)
        ev_qks_ct[ct] = (sem_ev, sem_ev.v)

        def pe_denom(e, ct=ct):
            W.wait(e, "pe", ev_qks_ct[ct])
            W.wait(e, "pe", dma_marks["consts"])
            if ct == 0:
                W.wait(e, "pe", (sem_ev, ev_vkevac))
            for half in range(2):
                ins = e.matmul(pb[2 + half][0:64, 0:512],
                               onesblk[:, 64 * ct:64 * ct + 64],
                               qks[ct % 4][:, half * 512:half * 512 + 512],
                               start=(ct == 0), stop=(ct == CT - 1))
            sem_mm.inc(ins)
        sec("tensor", pe_denom)
        qks_free[ct % 4] = (sem_mm, sem_mm.v)
    mm_denom = sem_mm.v

    def ac_recip(e):
        W.wait(e, "ac", (sem_mm, mm_denom))
        for half in range(2):
            ins = e.activation(stag[0:64, half * 512:half * 512 + 512],
                               pb[2 + half][0:64, 0:512], AF.Ln, bias=eps64[:, :])
        sem_ac.inc(ins)
        W.wait(e, "ac", (sem_ac, sem_ac.v))
        sem_ac.inc(e.activation(stag[0:64, 0:1024], stag[0:64, 0:1024], AF.Exp, scale=-1.0))
    sem_ac.v += 0
    sec("scalar", ac_recip)
    ac_recip_done = sem_ac.v

    def dv_recipb(e):
        W.wait(e, "ev", (sem_ac, ac_recip_done))
        sem_ev.inc(e.tensor_copy(recipb[:, :], stag[0:64, 0:1024]))
    sec("vector", dv_recipb)
    ev_recipb = sem_ev.v

    rbuf = [lnsq[0][:, 0:512], lnsq[1][:, 0:512]]
    rbuf_free = [None, None]
    bcb_free = [None, None]
    outb_free = [None, None]
    for half in range(2):
        for ct in range(CT):
            i = half * CT + ct

            def pe_bo(e, ct=ct, half=half, i=i):
                W.wait(e, "pe", (sem_ev, ev_recipb))
                W.wait(e, "pe", (sem_ev, ev_strips))
                W.wait(e, "pe", acq_ft[ct])
                W.wait(e, "pe", bcb_free[i % 2])
                W.wait(e, "pe", outb_free[i % 2])
                e.matmul(pb[6 + i % 2][:, 0:512],
                         selpack[:, 128 * ct:128 * ct + 128],
                         recipb[:, half * 512:half * 512 + 512],
                         start=True, stop=True)
                sem_mm.inc(e.matmul(pb[4 + i % 2][:, 0:512],
                                    strips[:, 128 * ct:128 * ct + 128],
                                    qF[ct][:, half * 512:half * 512 + 512],
                                    start=True, stop=True))
            sec("tensor", pe_bo)
            mm_bo = sem_mm.v

            def ac_rb(e, i=i, mm_bo=mm_bo):
                W.wait(e, "ac", (sem_mm, mm_bo))
                W.wait(e, "ac", rbuf_free[i % 2])
                sem_ac.inc(e.activation(rbuf[i % 2], pb[6 + i % 2][:, 0:512],
                                        AF.Relu))
            sec("scalar", ac_rb)
            ac_r = sem_ac.v
            bcb_free[i % 2] = (sem_ac, ac_r)

            def dv_attnT(e, ct=ct, half=half, i=i, mm_bo=mm_bo, ac_r=ac_r):
                W.wait(e, "ev", (sem_mm, mm_bo))
                W.wait(e, "ev", (sem_ac, ac_r))
                sem_ev.inc(e.tensor_tensor(attnT[ct][:, half * 512:half * 512 + 512],
                                           rbuf[i % 2], pb[4 + i % 2][:, 0:512],
                                           ALU.mult))
            sec("vector", dv_attnT)
            rbuf_free[i % 2] = (sem_ev, sem_ev.v)
            outb_free[i % 2] = (sem_ev, sem_ev.v)
    ev_attnT_done = sem_ev.v

    lnsq_free = [None, None]
    lnst = {}
    outstg_free = [None, None]

    def emit_ln_sq(ft, ready):
        def ac_sq(e, ft=ft, ready=ready):
            W.wait(e, "ac", ready)
            W.wait(e, "ac", lnsq_free[ft % 2])
            sem_ac.inc(e.activation(lnsq[ft % 2][:, :],
                                    xs[:, ft * NTOK: ft * NTOK + NTOK], AF.Square))
        sec("scalar", ac_sq)
        lnst[ft] = (sem_ac, sem_ac.v)

    def emit_ln_sums(ft, extra=None):
        def pe_sums(e, ft=ft, extra=extra):
            W.wait(e, "pe", lnst[ft])
            W.wait(e, "pe", (sem_ev, ev_init))
            W.wait(e, "pe", extra)
            for h in range(2):
                e.matmul(pb[4 + 2 * h][0:1, 0:512], ones_col,
                         xs[:, ft * NTOK + h * 512: ft * NTOK + h * 512 + 512],
                         start=(ft == 0), stop=(ft == CT - 1))
                ins = e.matmul(pb[5 + 2 * h][0:1, 0:512], ones_col,
                               lnsq[ft % 2][:, h * 512: h * 512 + 512],
                               start=(ft == 0), stop=(ft == CT - 1))
            sem_mm.inc(ins)
        sec("tensor", pe_sums)
        lnsq_free[ft % 2] = (sem_mm, sem_mm.v)
        return sem_mm.v

    def emit_ln_tail(dest_kind, sums_mark):
        A = lrow[:, 0:512]
        B = lrow[:, 512:1024]
        T = lrow[:, 1024:1536]
        for h in range(2):
            def dv_pre(e, h=h, sums_mark=sums_mark):
                W.wait(e, "ev", (sem_mm, sums_mark))
                e.tensor_scalar(A, pb[4 + 2 * h][0:1, 0:512], 1.0 / C, None,
                                ALU.mult, ALU.bypass)
                sem_ev.inc(e.tensor_scalar(B, pb[5 + 2 * h][0:1, 0:512], 1.0 / C,
                                           None, ALU.mult, ALU.bypass))
                W.wait(e, "ev", (sem_ev, sem_ev.v))
                sem_ev.inc(e.tensor_tensor(T, A, A, ALU.mult))
                W.wait(e, "ev", (sem_ev, sem_ev.v))
                sem_ev.inc(e.tensor_tensor(B, B, T, ALU.subtract))
            sec("vector", dv_pre)
            ev_var = sem_ev.v

            def ac_rstd(e, ev_var=ev_var):
                W.wait(e, "ac", (sem_ev, ev_var))
                sem_ac.inc(e.activation(T, B, AF.Ln, bias=eps1[:, :]))
                W.wait(e, "ac", (sem_ac, sem_ac.v))
                sem_ac.inc(e.activation(B, T, AF.Exp, scale=-0.5))
            sec("scalar", ac_rstd)
            ac_rr = sem_ac.v

            def dv_rows(e, h=h, ac_rr=ac_rr):
                W.wait(e, "ev", (sem_ac, ac_rr))
                sem_ev.inc(e.tensor_tensor(T, A, B, ALU.mult))
                W.wait(e, "ev", (sem_ev, sem_ev.v))
                e.tensor_copy(lnbf[:, 1024 * h:1024 * h + 512], B)
                sem_ev.inc(e.tensor_copy(lnbf[:, 1024 * h + 512:1024 * h + 1024], T))
            sec("vector", dv_rows)
        ev_rows = sem_ev.v

        def pe_bcast_ln(e, ev_rows=ev_rows):
            W.wait(e, "pe", (sem_ev, ev_rows))
            for h in range(2):
                e.matmul(pb[4 + 2 * h][:, 0:512], onesrow[:, :],
                         lnbf[0:1, 1024 * h:1024 * h + 512], start=True, stop=True)
                ins = e.matmul(pb[5 + 2 * h][:, 0:512], onesrow[:, :],
                               lnbf[0:1, 1024 * h + 512:1024 * h + 1024],
                               start=True, stop=True)
            sem_mm.inc(ins)
        sec("tensor", pe_bcast_ln)
        mm_bcast = sem_mm.v

        gstg_free = [None, None]
        half_marks = []
        for h in range(2):
            for ct in range(CT):
                def dv_apply(e, ct=ct, h=h, mm_bcast=mm_bcast,
                             dest_kind=dest_kind):
                    W.wait(e, "ev", (sem_mm, mm_bcast))
                    src = xs[:, ct * NTOK + h * 512: ct * NTOK + h * 512 + 512]
                    scr = lnsq[ct % 2][:, 0:512]
                    e.tensor_tensor(scr, src, pb[4 + 2 * h][:, 0:512], ALU.mult)
                    if dest_kind == "xs":
                        sem_ev.inc(e.tensor_tensor(src, scr,
                                                   pb[5 + 2 * h][:, 0:512],
                                                   ALU.subtract))
                    else:
                        stg = outstg[ct // 2 % 2] if ct % 2 == 0 else outstg_g[ct // 2 % 2]
                        W.wait(e, "ev", outstg_free[ct // 2 % 2] if ct % 2 == 0
                               else gstg_free[ct // 2 % 2])
                        sem_ev.inc(e.tensor_tensor(stg[:, 0:512], scr,
                                                   pb[5 + 2 * h][:, 0:512],
                                                   ALU.subtract))
                sec("vector", dv_apply)
                if dest_kind == "out":
                    ev_ap = sem_ev.v
                    if ct % 2 == 0:
                        osem = sem_oA if ct % 4 == 0 else sem_oB

                        def sy_store(e, ct=ct, h=h, ev_ap=ev_ap, osem=osem):
                            W.wait(e, "sy", (sem_ev, ev_ap))
                            e.dma_start(
                                out=outd[:, ct * NTOK + h * 512: ct * NTOK + h * 512 + 512],
                                in_=outstg[ct // 2 % 2][:, 0:512]).then_inc(osem.h, 16)
                        osem.v += 16
                        sec("sync", sy_store)
                        outstg_free[ct // 2 % 2] = (osem, osem.v)
                    else:
                        def gp_store(e, ct=ct, h=h, ev_ap=ev_ap):
                            W.wait(e, "gp", (sem_ev, ev_ap))
                            e.dma_start(
                                out=outd[:, ct * NTOK + h * 512: ct * NTOK + h * 512 + 512],
                                in_=outstg_g[ct // 2 % 2][:, 0:512]).then_inc(sem_oG.h, 16)
                        sem_oG.v += 16
                        sec("gpsimd", gp_store)
                        gstg_free[ct // 2 % 2] = (sem_oG, sem_oG.v)
            half_marks.append([(sem_ev, sem_ev.v), (sem_gv, sem_gv.v)])
        return half_marks

    projb_free = [None, None]
    for ft in range(CT):
        slot = ft % 2
        for half in range(2):
            i = ft * 2 + half
            qb = i % 2

            def pe_proj(e, ft=ft, half=half, qb=qb, slot=slot):
                W.wait(e, "pe", dma_marks[f"wp{ft}"])
                W.wait(e, "pe", (sem_ev, ev_attnT_done))
                W.wait(e, "pe", projb_free[qb])
                for cc in range(CT):
                    ins = e.matmul(pb[qb][:, :],
                                   wbuf[slot][:, 2048 + cc * 128: 2048 + cc * 128 + 128],
                                   attnT[cc][:, half * 512: half * 512 + 512],
                                   start=(cc == 0), stop=(cc == CT - 1))
                sem_mm.inc(ins)
            sec("tensor", pe_proj)
            mm_p = sem_mm.v

            def dv_proj(e, ft=ft, half=half, qb=qb, mm_p=mm_p):
                W.wait(e, "ev", (sem_mm, mm_p))
                W.wait(e, "ev", (sem_ds, ds_consts))
                dst = xs[:, ft * NTOK + half * 512: ft * NTOK + half * 512 + 512]
                sem_ev.inc(e.scalar_tensor_tensor(dst, pb[qb][:, :], bias_sb[:, ft:ft + 1],
                                                  dst, ALU.add, ALU.add))
            sec("vector", dv_proj)
            projb_free[qb] = (sem_ev, sem_ev.v)
        emit_ln_sq(ft, (sem_ev, sem_ev.v))
        if ft >= 1:
            emit_ln_sums(ft - 1,
                         extra=([bcb_free[0], bcb_free[1]] if ft == 1 else None))
        slot_free[slot] = (sem_mm, sem_mm.v)
        if ft + 2 < CT:
            issue_wdma(f"wp{ft + 2}", slot, wbuf[slot][:, 2048:4096], wproj[ft + 2, :, :])
    mm_sums1 = emit_ln_sums(CT - 1)
    ev_ln1_half = emit_ln_tail("xs", mm_sums1)



    gub_free = [None, None, None, None]
    h_done = {}
    ev_sparsify = {}
    for dt in range(NDT):
        slot = dt % 2
        for half in range(2):
            i = dt * 2 + half
            gi = i % 2

            goff = 4096 if dt < 2 else 0
            uoff = 6144 if dt < 2 else 2048

            def pe_gu(e, dt=dt, half=half, slot=slot, gi=gi, goff=goff, uoff=uoff):
                W.wait(e, "pe", dma_marks["wgu01" if dt < 2 else f"wg{dt}"])
                W.wait(e, "pe", ev_ln1_half[half])
                W.wait(e, "pe", gub_free[gi])
                W.wait(e, "pe", gub_free[2 + gi])
                for cc in range(CT):
                    e.matmul(pb[gi][:, :],
                             wbuf[slot][:, goff + cc * 128: goff + cc * 128 + 128],
                             xs[:, cc * NTOK + half * 512: cc * NTOK + half * 512 + 512],
                             start=(cc == 0), stop=(cc == CT - 1))
                if dt >= 2:
                    W.wait(e, "pe", dma_marks[f"wu{dt}"])
                for cc in range(CT):
                    ins = e.matmul(pb[2 + gi][:, :],
                                   wbuf[slot][:, uoff + cc * 128: uoff + cc * 128 + 128],
                                   xs[:, cc * NTOK + half * 512: cc * NTOK + half * 512 + 512],
                                   start=(cc == 0), stop=(cc == CT - 1))
                sem_mm.inc(ins)
            sec("tensor", pe_gu)
            mm_gu = sem_mm.v

            def ac_silu(e, i=i, gi=gi, mm_gu=mm_gu):
                W.wait(e, "ac", (sem_mm, mm_gu))
                sem_ac.inc(e.activation(lnsq[gi][:, 512:1024], pb[gi][:, :], AF.Silu))
            sec("scalar", ac_silu)
            ac_si = sem_ac.v

            def dv_h(e, dt=dt, half=half, gi=gi, ac_si=ac_si):
                W.wait(e, "ev", (sem_ac, ac_si))
                sem_ev.inc(e.tensor_tensor(hst[dt][:, half * 512: half * 512 + 512],
                                           lnsq[gi][:, 512:1024], pb[2 + gi][:, :], ALU.mult))
            sec("vector", dv_h)
            gub_free[gi] = (sem_ev, sem_ev.v)
            gub_free[2 + gi] = (sem_ev, sem_ev.v)
        h_done[dt] = (sem_ev, sem_ev.v)
        slot_free[slot] = (sem_mm, sem_mm.v)
        if dt + 2 < NDT:
            issue_wdma(f"wg{dt + 2}", slot, wbuf[slot][:, 0:2048], wg[dt + 2, :, :])
            issue_wdma(f"wu{dt + 2}", slot, wbuf[slot][:, 2048:4096], wu[dt + 2, :, :])

        if dt % 4 == 3:
            a = dt // 4

            def dv_sparsify(e, a=a):
                W.wait(e, "ev", h_done[4 * a + 3])
                s, t = ssc, tsc
                hj = [hst[4 * a + j] for j in range(4)]
                for j in range(3):
                    e.tensor_tensor(s[j][:, :], hj[j][:, :], hj[j][:, :], ALU.mult)
                sem_ev.inc(e.tensor_tensor(s[3][:, :], hj[3][:, :], hj[3][:, :], ALU.mult))
                W.wait(e, "ev", (sem_ev, sem_ev.v))
                e.tensor_tensor(t[0][:, :], s[0][:, :], s[1][:, :], ALU.max)
                e.tensor_tensor(t[1][:, :], s[0][:, :], s[1][:, :], ALU.min)
                e.tensor_tensor(t[2][:, :], s[2][:, :], s[3][:, :], ALU.max)
                sem_ev.inc(e.tensor_tensor(t[3][:, :], s[2][:, :], s[3][:, :], ALU.min))
                W.wait(e, "ev", (sem_ev, sem_ev.v))
                e.tensor_tensor(t[1][:, :], t[1][:, :], t[3][:, :], ALU.max)
                sem_ev.inc(e.tensor_tensor(t[0][:, :], t[0][:, :], t[2][:, :], ALU.min))
                W.wait(e, "ev", (sem_ev, sem_ev.v))
                sem_ev.inc(e.tensor_tensor(t[0][:, :], t[0][:, :], t[1][:, :], ALU.max))
                W.wait(e, "ev", (sem_ev, sem_ev.v))
                for j in range(3):
                    e.tensor_tensor(s[j][:, :], s[j][:, :], t[0][:, :], ALU.is_ge)
                sem_ev.inc(e.tensor_tensor(s[3][:, :], s[3][:, :], t[0][:, :], ALU.is_ge))
                W.wait(e, "ev", (sem_ev, sem_ev.v))
                for j in range(3):
                    e.tensor_tensor(hj[j][:, :], hj[j][:, :], s[j][:, :], ALU.mult)
                sem_ev.inc(e.tensor_tensor(hj[3][:, :], hj[3][:, :], s[3][:, :], ALU.mult))
            sec("vector", dv_sparsify)
            ev_sparsify[a] = (sem_ev, sem_ev.v)
    ev_sparsify_done = sem_ev.v

    def gp_wd01(e):
        W.wait(e, "gp", slot_free[0])
        W.wait(e, "gp", slot_free[1])
        e.dma_start(out=wbuf[0][:, 0:4096], in_=wd[0, :, 0:4096],
                    max_dma_last_dim=2048).then_inc(sem_dc.h, 16)
        e.dma_start(out=wbuf[1][:, 0:4096], in_=wd[1, :, 0:4096],
                    max_dma_last_dim=2048).then_inc(sem_dc.h, 16)
    sem_dc.v += 32
    dma_marks["wd01_lo"] = (sem_dc, sem_dc.v)
    sec("gpsimd", gp_wd01)

    def gp_wd01h(e):
        W.wait(e, "gp", (sem_ev, ev_sparsify_done))
        e.dma_start(out=wbuf[0][:, 4096:8192], in_=wd[0, :, 4096:8192],
                    max_dma_last_dim=2048).then_inc(sem_dc.h, 16)
        e.dma_start(out=wbuf[1][:, 4096:8192], in_=wd[1, :, 4096:8192],
                    max_dma_last_dim=2048).then_inc(sem_dc.h, 16)
    sem_dc.v += 32
    dma_marks["wd01_hi"] = (sem_dc, sem_dc.v)
    sec("gpsimd", gp_wd01h)
    dma_marks["wd0"] = dma_marks["wd01_lo"]
    dma_marks["wd1"] = dma_marks["wd01_lo"]

    downb_free = [None, None, None, None]
    lnsq2_free = {}
    ln2_sums_mark = {}
    A2s = lrow[:, 0:512]
    B2s = lrow[:, 512:1024]
    T2s = lrow[:, 1024:1536]
    out_free = {"sA": [None, None], "sG": [None, None]}

    def ln2_sq(ft, h, ready):
        def ac_sq2(e, ft=ft, h=h, ready=ready):
            W.wait(e, "ac", ready)
            W.wait(e, "ac", lnsq2_free.get((ft % 2, h)))
            sem_ac.inc(e.activation(
                lnsq[ft % 2][:, h * 512: h * 512 + 512],
                xs[:, ft * NTOK + h * 512: ft * NTOK + h * 512 + 512],
                AF.Square))
        sec("scalar", ac_sq2)
        lnst[(ft, h)] = (sem_ac, sem_ac.v)

    def ln2_sums(ft, h, extra=None):
        def pe_s2(e, ft=ft, h=h, extra=extra):
            W.wait(e, "pe", lnst[(ft, h)])
            W.wait(e, "pe", extra)
            e.matmul(pb[4 + 2 * h][0:1, 0:512], ones_col,
                     xs[:, ft * NTOK + h * 512: ft * NTOK + h * 512 + 512],
                     start=(ft == 0), stop=(ft == CT - 1))
            ins = e.matmul(pb[5 + 2 * h][0:1, 0:512], ones_col,
                           lnsq[ft % 2][:, h * 512: h * 512 + 512],
                           start=(ft == 0), stop=(ft == CT - 1))
            sem_mm.inc(ins)
        sec("tensor", pe_s2)
        lnsq2_free[(ft % 2, h)] = (sem_mm, sem_mm.v)
        ln2_sums_mark[h] = (sem_mm, sem_mm.v)

    def ln2_stats(h):
        smark = ln2_sums_mark[h]

        def dv_pre2(e, h=h, smark=smark):
            W.wait(e, "ev", smark)
            e.tensor_scalar(A2s, pb[4 + 2 * h][0:1, 0:512], 1.0 / C, None,
                            ALU.mult, ALU.bypass)
            sem_ev.inc(e.tensor_scalar(B2s, pb[5 + 2 * h][0:1, 0:512], 1.0 / C,
                                       None, ALU.mult, ALU.bypass))
            W.wait(e, "ev", (sem_ev, sem_ev.v))
            sem_ev.inc(e.tensor_tensor(T2s, A2s, A2s, ALU.mult))
            W.wait(e, "ev", (sem_ev, sem_ev.v))
            sem_ev.inc(e.tensor_tensor(B2s, B2s, T2s, ALU.subtract))
        sec("vector", dv_pre2)
        ev_var = sem_ev.v

        def ac_rstd2(e, ev_var=ev_var):
            W.wait(e, "ac", (sem_ev, ev_var))
            sem_ac.inc(e.activation(T2s, B2s, AF.Ln, bias=eps1[:, :]))
            W.wait(e, "ac", (sem_ac, sem_ac.v))
            sem_ac.inc(e.activation(B2s, T2s, AF.Exp, scale=-0.5))
        sec("scalar", ac_rstd2)
        ac_rr = sem_ac.v

        def dv_rows2(e, h=h, ac_rr=ac_rr):
            W.wait(e, "ev", (sem_ac, ac_rr))
            sem_ev.inc(e.tensor_tensor(T2s, A2s, B2s, ALU.mult))
            W.wait(e, "ev", (sem_ev, sem_ev.v))
            e.tensor_copy(lnbf[:, 1024 * h:1024 * h + 512], B2s)
            sem_ev.inc(e.tensor_copy(lnbf[:, 1024 * h + 512:1024 * h + 1024], T2s))
        sec("vector", dv_rows2)
        return sem_ev.v

    def ln2_bcast(h, ev_rows):
        def pe_b2(e, h=h, ev_rows=ev_rows):
            W.wait(e, "pe", (sem_ev, ev_rows))
            e.matmul(pb[4 + 2 * h][:, 0:512], onesrow[:, :],
                     lnbf[0:1, 1024 * h:1024 * h + 512], start=True, stop=True)
            ins = e.matmul(pb[5 + 2 * h][:, 0:512], onesrow[:, :],
                           lnbf[0:1, 1024 * h + 512:1024 * h + 1024],
                           start=True, stop=True)
            sem_mm.inc(ins)
        sec("tensor", pe_b2)
        return sem_mm.v

    def ln2_apply(h, mm_b, cts):
        for ct in cts:
            def dv_ap2(e, ct=ct, h=h, mm_b=mm_b):
                W.wait(e, "ev", (sem_mm, mm_b))
                src = xs[:, ct * NTOK + h * 512: ct * NTOK + h * 512 + 512]
                scr = lnsq[ct % 2][:, 0:512]
                e.tensor_tensor(scr, src, pb[4 + 2 * h][:, 0:512], ALU.mult)
                stg = outstg[ct // 2 % 2] if ct % 2 == 0 else outstg_g[ct // 2 % 2]
                W.wait(e, "ev", out_free["sA" if ct % 2 == 0 else "sG"][ct // 2 % 2])
                sem_ev.inc(e.tensor_tensor(stg[:, 0:512], scr,
                                           pb[5 + 2 * h][:, 0:512], ALU.subtract))
            sec("vector", dv_ap2)
            ev_ap = sem_ev.v
            if ct % 2 == 0:
                osem = sem_oA if ct % 4 == 0 else sem_oB

                def sy_st2(e, ct=ct, h=h, ev_ap=ev_ap, osem=osem):
                    W.wait(e, "sy", (sem_ev, ev_ap))
                    e.dma_start(
                        out=outd[:, ct * NTOK + h * 512: ct * NTOK + h * 512 + 512],
                        in_=outstg[ct // 2 % 2][:, 0:512]).then_inc(osem.h, 16)
                osem.v += 16
                sec("sync", sy_st2)
                out_free["sA"][ct // 2 % 2] = (osem, osem.v)
            else:
                def gp_st2(e, ct=ct, h=h, ev_ap=ev_ap):
                    W.wait(e, "gp", (sem_ev, ev_ap))
                    e.dma_start(
                        out=outd[:, ct * NTOK + h * 512: ct * NTOK + h * 512 + 512],
                        in_=outstg_g[ct // 2 % 2][:, 0:512]).then_inc(sem_oG.h, 16)
                sem_oG.v += 16
                sec("gpsimd", gp_st2)
                out_free["sG"][ct // 2 % 2] = (sem_oG, sem_oG.v)

    def pe_down_half(e, ft, slot, b, h, dts):
        if h == 0 and dts[0] == 0:
            W.wait(e, "pe", dma_marks[f"wd{ft}"])
        if dts[0] == 0:
            W.wait(e, "pe", downb_free[b])
        for dt in dts:
            W.wait(e, "pe", ev_sparsify[dt // 4])
            if ft < 2 and dt == 32:
                W.wait(e, "pe", dma_marks["wd01_hi"])
            ins = e.matmul(pb[b][:, :], wbuf[slot][:, dt * 128: dt * 128 + 128],
                           hst[dt][:, h * 512: h * 512 + 512],
                           start=(dt == 0), stop=(dt == NDT - 1))
        sem_mm.inc(ins)

    for ft in range(CT):
        slot = ft % 2
        b0 = 2 * (ft % 2)
        for h in range(2):
            b = b0 + h
            if ft == CT - 1 and h == 1:
                break
            def pe_dh(e, ft=ft, slot=slot, b=b, h=h):
                pe_down_half(e, ft, slot, b, h, list(range(NDT)))
            sec("tensor", pe_dh)
            mm_d = sem_mm.v

            def dv_dh(e, ft=ft, b=b, h=h, mm_d=mm_d):
                W.wait(e, "ev", (sem_mm, mm_d))
                dst = xs[:, ft * NTOK + h * 512: ft * NTOK + h * 512 + 512]
                sem_ev.inc(e.tensor_tensor(dst, pb[b][:, :], dst, ALU.add))
            sec("vector", dv_dh)
            downb_free[b] = (sem_ev, sem_ev.v)
            ln2_sq(ft, h, (sem_ev, sem_ev.v))
            if h == 1:
                ln2_sums(ft, 0, extra=(ev_ln1_half[1] if ft == 0 else None))
            elif ft >= 1:
                ln2_sums(ft - 1, 1)
        if ft == CT - 1:
            break
        slot_free[slot] = (sem_mm, sem_mm.v)
        if ft + 2 < CT:
            issue_wdma(f"wd{ft + 2}", slot, wbuf[slot][:, 0:8192], wd[ft + 2, :, :])

    ln2_sums(CT - 1, 0)
    ev_rows0 = ln2_stats(0)
    b15 = 2 * ((CT - 1) % 2) + 1

    def pe_d15a(e):
        pe_down_half(e, CT - 1, (CT - 1) % 2, b15, 1, list(range(0, 32)))
    sec("tensor", pe_d15a)

    mm_b0 = ln2_bcast(0, ev_rows0)

    def pe_d15b(e):
        pe_down_half(e, CT - 1, (CT - 1) % 2, b15, 1, list(range(32, NDT)))
    sec("tensor", pe_d15b)
    mm_d15 = sem_mm.v

    ln2_apply(0, mm_b0, list(range(6)))

    def dv_d15(e, mm_d15=mm_d15):
        W.wait(e, "ev", (sem_mm, mm_d15))
        dst = xs[:, (CT - 1) * NTOK + 512: (CT - 1) * NTOK + 1024]
        sem_ev.inc(e.tensor_tensor(dst, pb[b15][:, :], dst, ALU.add))
    sec("vector", dv_d15)
    ln2_sq(CT - 1, 1, (sem_ev, sem_ev.v))
    ln2_sums(CT - 1, 1)
    ln2_apply(0, mm_b0, list(range(6, CT)))
    ev_rows1 = ln2_stats(1)
    mm_b1 = ln2_bcast(1, ev_rows1)
    ln2_apply(1, mm_b1, list(range(CT)))

    fA, fB, fG = sem_oA.v, sem_oB.v, sem_oG.v

    def sy_final(e):
        e.wait_ge(sem_oA.h, fA)
        e.wait_ge(sem_oB.h, fB)
        e.wait_ge(sem_oG.h, fG)
    sec("sync", sy_final)

    es.close()
    return nc


_NC_CACHE = None


def _get_nc():
    global _NC_CACHE
    if _NC_CACHE is None:
        _NC_CACHE = build_nc()
    return _NC_CACHE


def kernel(L0_attn_x, L0_attn_w_qkv, L0_attn_w_proj, L0_attn_b_proj,
           L0_ffn_w_gate, L0_ffn_w_up, L0_ffn_w_down):
    x = np.asarray(L0_attn_x, np.float32)
    mask128, selpack, onesblk = prep_consts()
    shared = {
        "wkv": prep_wkv(np.asarray(L0_attn_w_qkv, np.float32)),
        "wq": prep_wq(np.asarray(L0_attn_w_qkv, np.float32)),
        "wproj": prep_wproj(np.asarray(L0_attn_w_proj, np.float32)),
        "wg": prep_wgu(np.asarray(L0_ffn_w_gate, np.float32)),
        "wu": prep_wgu(np.asarray(L0_ffn_w_up, np.float32)),
        "wd": prep_wd(np.asarray(L0_ffn_w_down, np.float32)),
        "biasin": prep_bias(np.asarray(L0_attn_b_proj, np.float32)),
        "maskin": mask128,
        "selin": selpack,
        "obin": onesblk,
    }
    in_maps = []
    for core in range(NCORES):
        m = dict(shared)
        m["xh"] = prep_x(x, core)
        in_maps.append(m)

    nc = _get_nc()
    res = run_bass_kernel_spmd(nc, in_maps, core_ids=list(range(NCORES)))
    rows = [unprep_out(np.asarray(res.results[c]["outd"]), c) for c in range(NCORES)]
    return np.concatenate(rows, axis=0)[None, :, :].astype(np.float32)
